# version 14
# baseline (speedup 1.0000x reference)
import os
import numpy as np
import ml_dtypes
from contextlib import ExitStack

import concourse.bass as bass
import concourse.mybir as mybir
from concourse.bass_utils import run_bass_kernel_spmd

F32 = mybir.dt.float32
BF16 = mybir.dt.bfloat16
E4 = mybir.dt.float8e4
AF = mybir.ActivationFunctionType
ALU = mybir.AluOpType
DR = mybir.MatmulPerfMode.DoubleRow

T = 2048
D = 2048
HID = 8192
NDC = D // 128
NHC = HID // 128
NS = 4
NTH = 2
NTBQ = 4

NQ = 4
ENG_OF = {"pe": "tensor", "dve": "vector", "act": "scalar"}
for _i in range(NQ):
    ENG_OF[f"sy{_i}"] = "sync"
    ENG_OF[f"gp{_i}"] = "gpsimd"
ALL_SEMS = list(ENG_OF.keys())


class Res:
    __slots__ = ("w", "r")

    def __init__(self):
        self.w = {}
        self.r = {}


class Builder:
    def __init__(self):
        self.cmds = {e: [] for e in ("tensor", "vector", "scalar", "sync", "gpsimd")}
        self.cnt = {s: 0 for s in ALL_SEMS + ["bar"]}
        self.sem = {}
        self.waited = {e: {} for e in self.cmds}
        self.nbar = 0
        self.rr = {"sy": 0, "gp": 0}

    def _wait(self, eng, sem, val):
        if val <= 0:
            return
        if self.waited[eng].get(sem, 0) >= val:
            return
        self.waited[eng][sem] = val
        s = self.sem[sem]
        self.cmds[eng].append(lambda e, s=s, v=val: e.wait_ge(s, v))

    def _edges(self, eng, reads, writes):
        for r in reads:
            for sm, v in r.w.items():
                self._wait(eng, sm, v)
        for r in writes:
            for sm, v in r.r.items():
                self._wait(eng, sm, v)
            for sm, v in r.w.items():
                self._wait(eng, sm, v)

    def _record(self, sem, v, reads, writes):
        for r in reads:
            r.r[sem] = v
        for r in writes:
            r.w = {sem: v}
            r.r = {}

    def op(self, sem, build, reads=(), writes=()):
        eng = ENG_OF[sem]
        self._edges(eng, reads, writes)
        self.cnt[sem] += 1
        v = self.cnt[sem]
        s = self.sem[sem]
        self.cmds[eng].append(lambda e, b=build, s=s: b(e).then_inc(s, 1))
        self._record(sem, v, reads, writes)
        return v

    def op_group(self, sem, builds, reads=(), writes=()):
        eng = ENG_OF[sem]
        self._edges(eng, reads, writes)
        self.cnt[sem] += 1
        v = self.cnt[sem]
        s = self.sem[sem]
        n = len(builds)
        for i, bd in enumerate(builds):
            if i == n - 1:
                self.cmds[eng].append(lambda e, b=bd, s=s: b(e).then_inc(s, 1))
            else:
                self.cmds[eng].append(lambda e, b=bd: b(e))
        self._record(sem, v, reads, writes)
        return v

    def dma(self, q, out_ap, in_ap, reads=(), writes=()):
        sem = f"{q}{self.rr[q]}"
        self.rr[q] = (self.rr[q] + 1) % NQ
        eng = ENG_OF[sem]
        self._wait(eng, sem, self.cnt[sem])
        self._edges(eng, reads, writes)
        self.cnt[sem] += 16
        v = self.cnt[sem]
        s = self.sem[sem]
        self.cmds[eng].append(
            lambda e, o=out_ap, i=in_ap, s=s: e.dma_start(out=o, in_=i).then_inc(s, 16))
        self._record(sem, v, reads, writes)
        return v

    def barrier(self):
        self.nbar += 1
        tgt = 5 * self.nbar
        bar = self.sem["bar"]
        for eng in self.cmds:
            for sm in ALL_SEMS:
                if ENG_OF[sm] == eng:
                    self._wait(eng, sm, self.cnt[sm])
            self.cmds[eng].append(lambda e, b=bar: e.sem_inc(b, 1))
            self._wait(eng, "bar", tgt)


def build_nc(phases=(0, 1, 2, 3)):
    nc = bass.Bass("TRN2", target_bir_lowering=False)

    x_d = nc.dram_tensor("x", [T, D], F32, kind="ExternalInput").ap()
    winT_d = nc.dram_tensor("winT", [NDC, 128, D], F32, kind="ExternalInput").ap()
    woutT_d = nc.dram_tensor("woutT", [NDC, 128, D], F32, kind="ExternalInput").ap()
    winraw_d = nc.dram_tensor("winraw", [D, D], F32, kind="ExternalInput").ap()
    wdtT_d = nc.dram_tensor("wdtT", [128, NDC * 32], F32, kind="ExternalInput").ap()
    bdt_d = nc.dram_tensor("bdt", [32, 1], F32, kind="ExternalInput").ap()
    alog_d = nc.dram_tensor("alog", [128, 16], F32, kind="ExternalInput").ap()
    bsel_d = nc.dram_tensor("bsel", [32, 16 * 128], BF16, kind="ExternalInput").ap()
    ident_d = nc.dram_tensor("ident", [128, 128], F32, kind="ExternalInput").ap()
    wg0_d = nc.dram_tensor("wg0", [NHC, 128, D], F32, kind="ExternalInput").ap()
    wg1_d = nc.dram_tensor("wg1", [NHC, 128, D], F32, kind="ExternalInput").ap()
    wlin_d = nc.dram_tensor("wlin", [NDC, 128, HID], F32, kind="ExternalInput").ap()
    out_d = nc.dram_tensor("out", [T, D], F32, kind="ExternalOutput").ap()
    xnf_d = nc.dram_tensor("xnf_s", [NDC, 128, T], F32, kind="Internal").ap()
    g_d = nc.dram_tensor("g_sc", [NDC, 128, T], BF16, kind="Internal").ap()
    acts_d = nc.dram_tensor("acts_s", [4, 128, NHC * 512], E4, kind="Internal").ap()
    wlinq_d = nc.dram_tensor("wlinq_s", [NDC, 128, HID], E4, kind="Internal").ap()
    winh_d = nc.dram_tensor("winh_s", [NDC, 128, D], BF16, kind="Internal").ap()
    winl_d = nc.dram_tensor("winl_s", [NDC, 128, D], BF16, kind="Internal").ap()
    wouth_d = nc.dram_tensor("wouth_s", [NDC, 128, D], BF16, kind="Internal").ap()
    woutl_d = nc.dram_tensor("woutl_s", [NDC, 128, D], BF16, kind="Internal").ap()

    b = Builder()
    es_glob = ExitStack()
    for sname in ALL_SEMS + ["bar"]:
        b.sem[sname] = es_glob.enter_context(nc.semaphore("S_" + sname))

    es_g2 = ExitStack()
    identb = es_g2.enter_context(nc.sbuf_tensor("identb", [128, 128], F32))
    wdt_h = es_g2.enter_context(nc.sbuf_tensor("wdt_h", [128, NDC * 32], BF16))
    wdt_l = es_g2.enter_context(nc.sbuf_tensor("wdt_l", [128, NDC * 32], BF16))
    wct_h = es_g2.enter_context(nc.sbuf_tensor("wct_h", [128, NDC * 32], BF16))
    wct_l = es_g2.enter_context(nc.sbuf_tensor("wct_l", [128, NDC * 32], BF16))
    bdt_t = es_g2.enter_context(nc.sbuf_tensor("bdt_t", [32, 1], F32))
    negA = es_g2.enter_context(nc.sbuf_tensor("negA", [128, 16], F32))
    bsel_t = es_g2.enter_context(nc.sbuf_tensor("bsel_t", [32, 16 * 128], BF16))
    epst = es_g2.enter_context(nc.sbuf_tensor("epst", [128, 1], F32))
    onet = es_g2.enter_context(nc.sbuf_tensor("onet", [32, 1], F32))
    state = es_g2.enter_context(nc.sbuf_tensor("state", [128, 16], F32))

    R = {}
    for nm in ("identb", "wdt_h", "wdt_l", "wct_h", "wct_l", "bdt_t", "negA",
               "bsel_t", "epst", "onet", "state", "q", "acts",
               "g_dr", "out_dr", "xnf_dr", "wlinq_dr", "winhl_dr", "wouthl_dr"):
        R[nm] = Res()

    b.dma("sy", identb[:], ident_d, writes=[R["identb"]])
    b.dma("sy", bdt_t[:], bdt_d, writes=[R["bdt_t"]])
    b.dma("sy", negA[:], alog_d, writes=[R["negA"]])
    b.dma("sy", bsel_t[:], bsel_d, writes=[R["bsel_t"]])
    b.op("dve", lambda e: e.memset(epst[:], 1e-5), writes=[R["epst"]])
    b.op("dve", lambda e: e.memset(onet[:], 1.0), writes=[R["onet"]])
    b.op("act", lambda e: e.activation(out=negA[:], in_=negA[:], func=AF.Exp),
         reads=[R["negA"]], writes=[R["negA"]])
    b.op("dve", lambda e: e.tensor_scalar_mul(out=negA[:], in0=negA[:], scalar1=-1.0),
         reads=[R["negA"]], writes=[R["negA"]])

    es_q = ExitStack()
    q_t = es_q.enter_context(nc.sbuf_tensor("q_t", [128, NDC * T], E4, side="right"))

    es1 = ExitStack()
    xt = [es1.enter_context(nc.sbuf_tensor(f"xt{i}", [128, D], F32)) for i in range(2)]
    xn_t = [es1.enter_context(nc.sbuf_tensor(f"xn{i}", [128, D], F32)) for i in range(8)]
    bns = es1.enter_context(nc.sbuf_tensor("bns", [128, 8 * 6 * 4], F32))
    mv = es1.enter_context(nc.sbuf_tensor("mv", [128, 16], F32))
    rstd = es1.enter_context(nc.sbuf_tensor("rstd", [128, 8], F32))
    nmr = es1.enter_context(nc.sbuf_tensor("nmr", [128, 8], F32))
    xnf_st = es1.enter_context(nc.sbuf_tensor("xnf_st", [128, NDC * 512], F32))
    ps1 = [es1.enter_context(nc.psum_tensor(f"ps1_{i}", [128, 512], F32)) for i in range(4)]

    r_xt = [Res() for _ in range(2)]
    r_xn = [Res() for _ in range(8)]
    r_stats = [Res() for _ in range(8)]
    r_xnst = Res()
    r_ps1 = [Res() for _ in range(4)]

    if 0 in phases:
        es0 = ExitStack()
        wdt_f = es0.enter_context(nc.sbuf_tensor("wdt_f", [128, NDC * 32], F32))
        w0buf = [es0.enter_context(nc.sbuf_tensor(f"w0b{i}", [128, D], F32)) for i in range(1)]
        w0h = [es0.enter_context(nc.sbuf_tensor(f"w0h{i}", [128, D], BF16)) for i in range(2)]
        w0l = [es0.enter_context(nc.sbuf_tensor(f"w0l{i}", [128, D], BF16)) for i in range(2)]
        wc_f = es0.enter_context(nc.sbuf_tensor("wc_f", [32, D], F32))
        wct_f = es0.enter_context(nc.sbuf_tensor("wct_f", [128, NDC * 32], F32))
        pwc = [es0.enter_context(nc.psum_tensor(f"pwc{i}", [32, 512], F32)) for i in range(4)]
        r_wdtf = Res()
        r_w0buf = [Res(), Res()]
        r_w0h = [Res(), Res()]
        r_w0l = [Res(), Res()]
        r_wcf = Res()
        r_wctf = Res()
        r_pwc = [Res() for _ in range(4)]

        b.dma("sy", wdt_f[:], wdtT_d, writes=[r_wdtf])
        b.op("dve", lambda e: e.tensor_copy(out=wdt_h[:], in_=wdt_f[:]),
             reads=[r_wdtf], writes=[R["wdt_h"]])
        b.op("dve", lambda e: e.tensor_sub(out=wdt_l[:], in0=wdt_f[:], in1=wdt_h[:]),
             reads=[r_wdtf, R["wdt_h"]], writes=[R["wdt_l"]])
        for ns in range(4):
            builds = []
            k = 0
            nk = 3 * NDC
            for ec in range(NDC):
                p = ec % 2
                for (lhs, rhs) in ((wdt_h, w0h[p]), (wdt_h, w0l[p]), (wdt_l, w0h[p])):
                    builds.append(lambda e, lhs=lhs, rhs=rhs, ec=ec, ns=ns, k=k: e.matmul(
                        out=pwc[ns][:], lhsT=lhs[:, ec * 32:(ec + 1) * 32],
                        rhs=rhs[:, ns * 512:(ns + 1) * 512],
                        start=(k == 0), stop=(k == nk - 1)))
                    k += 1
            if ns == 0:
                wc_builds = [builds]
            else:
                wc_builds.append(builds)
        for ec in range(NDC):
            p = ec % 2
            b.dma("sy", w0buf[0][:], winraw_d[ec * 128:(ec + 1) * 128, :], writes=[r_w0buf[0]])
            b.op("dve", lambda e, p=p: e.tensor_copy(out=w0h[p][:], in_=w0buf[0][:]),
                 reads=[r_w0buf[0]], writes=[r_w0h[p]])
            b.op("dve", lambda e, p=p: e.tensor_sub(out=w0l[p][:], in0=w0buf[0][:], in1=w0h[p][:]),
                 reads=[r_w0buf[0], r_w0h[p]], writes=[r_w0l[p]])
            for ns in range(4):
                b.op_group("pe", wc_builds[ns][ec * 3:(ec + 1) * 3],
                           reads=[R["wdt_h"], R["wdt_l"], r_w0h[p], r_w0l[p]],
                           writes=[r_pwc[ns]])
        for ns in range(4):
            b.op("dve", lambda e, ns=ns: e.tensor_copy(out=wc_f[:, ns * 512:(ns + 1) * 512],
                                                       in_=pwc[ns][:]),
                 reads=[r_pwc[ns]], writes=[r_wcf])
        builds = []
        for ec in range(NDC):
            builds.append(lambda e, ec=ec: e.matmul(
                out=ps1[0][:, ec * 32:(ec + 1) * 32], lhsT=wc_f[:, ec * 128:(ec + 1) * 128],
                rhs=identb[0:32, 0:32], is_transpose=True,
                start=(ec == 0), stop=(ec == NDC - 1)))
        b.op_group("pe", builds, reads=[r_wcf, R["identb"]], writes=[r_ps1[0]])
        b.op("dve", lambda e: e.tensor_copy(out=wct_f[:], in_=ps1[0][:]),
             reads=[r_ps1[0]], writes=[r_wctf])
        b.op("dve", lambda e: e.tensor_copy(out=wct_h[:], in_=wct_f[:]),
             reads=[r_wctf], writes=[R["wct_h"]])
        b.op("dve", lambda e: e.tensor_sub(out=wct_l[:], in0=wct_f[:], in1=wct_h[:]),
             reads=[r_wctf, R["wct_h"]], writes=[R["wct_l"]])
        es0.close()


    for tbq in range(NTBQ if 1 in phases else 0):
        tp_ = tbq % 2
        for tb4 in range(4):
            tb = tbq * 4 + tb4
            p = tb % 2
            sl = tb4 + 4 * tp_
            b.dma("sy", xt[p][:], x_d[tb * 128:(tb + 1) * 128, :], writes=[r_xt[p]])
            for g in range(4):
                b.op("dve", lambda e, p=p, sl=sl, g=g: e.bn_stats(
                    out=bns[:, (sl * 4 + g) * 6:(sl * 4 + g + 1) * 6],
                    in_=xt[p][:, g * 512:(g + 1) * 512]),
                    reads=[r_xt[p]], writes=[r_stats[sl]])
            b.op("dve", lambda e, sl=sl: e.bn_aggr(
                out=mv[:, sl * 2:sl * 2 + 2], in_=bns[:, sl * 24:(sl + 1) * 24]),
                reads=[r_stats[sl]], writes=[r_stats[sl]])
            b.op("act", lambda e, sl=sl: e.activation(
                out=rstd[:, sl:sl + 1], in_=mv[:, sl * 2 + 1:sl * 2 + 2],
                func=AF.Sqrt, bias=epst[:]),
                reads=[r_stats[sl], R["epst"]], writes=[r_stats[sl]])
            b.op("dve", lambda e, sl=sl: e.reciprocal(
                out=rstd[:, sl:sl + 1], in_=rstd[:, sl:sl + 1]),
                reads=[r_stats[sl]], writes=[r_stats[sl]])
            b.op("dve", lambda e, sl=sl: e.tensor_mul(
                out=nmr[:, sl:sl + 1], in0=mv[:, sl * 2:sl * 2 + 1],
                in1=rstd[:, sl:sl + 1]),
                reads=[r_stats[sl]], writes=[r_stats[sl]])
            b.op("dve", lambda e, sl=sl: e.tensor_scalar_mul(
                out=nmr[:, sl:sl + 1], in0=nmr[:, sl:sl + 1], scalar1=-1.0),
                reads=[r_stats[sl]], writes=[r_stats[sl]])
            b.op("act", lambda e, p=p, sl=sl: e.activation(
                out=xn_t[sl][:], in_=xt[p][:], func=AF.Identity,
                scale=rstd[:, sl:sl + 1], bias=nmr[:, sl:sl + 1]),
                reads=[r_xt[p], r_stats[sl]], writes=[r_xn[sl]])
        for dc in range(NDC):
            pp = dc % 2 + 2 * tp_
            builds = []
            for tb4 in range(4):
                builds.append(lambda e, dc=dc, tb4=tb4, pp=pp, tp_=tp_: e.matmul(
                    out=ps1[pp][:, tb4 * 128:(tb4 + 1) * 128],
                    lhsT=xn_t[tb4 + 4 * tp_][:, dc * 128:(dc + 1) * 128],
                    rhs=identb[:], is_transpose=True,
                    start=(tb4 == 0), stop=(tb4 == 3)))
            b.op_group("pe", builds,
                       reads=[r_xn[4 * tp_], r_xn[1 + 4 * tp_], r_xn[2 + 4 * tp_],
                              r_xn[3 + 4 * tp_], R["identb"]],
                       writes=[r_ps1[pp]])
            b.op("dve", lambda e, dc=dc, pp=pp: e.tensor_copy(
                out=xnf_st[:, dc * 512:(dc + 1) * 512], in_=ps1[pp][:]),
                reads=[r_ps1[pp]], writes=[r_xnst])
            b.op("act", lambda e, dc=dc, tbq=tbq: e.copy(
                out=q_t[:, dc * T + tbq * 512: dc * T + (tbq + 1) * 512],
                in_=xnf_st[:, dc * 512:(dc + 1) * 512]),
                reads=[r_xnst], writes=[R["q"]])
        b.dma("gp", xnf_d[:, :, tbq * 512:(tbq + 1) * 512].rearrange("dc p t -> p dc t"),
              xnf_st[:], reads=[r_xnst], writes=[R["xnf_dr"]])

    es1.close()
    b.barrier()

    es2 = ExitStack()
    g0buf = [es2.enter_context(nc.sbuf_tensor(f"g0b{i}", [128, D], F32)) for i in range(3)]
    g1buf = [es2.enter_context(nc.sbuf_tensor(f"g1b{i}", [128, D], F32)) for i in range(3)]
    g0q = [es2.enter_context(nc.sbuf_tensor(f"g0q{i}", [128, D], E4)) for i in range(3)]
    g1q = [es2.enter_context(nc.sbuf_tensor(f"g1q{i}", [128, D], E4)) for i in range(3)]
    acts_hc = [es2.enter_context(nc.sbuf_tensor(f"acts{i}", [128, T], E4)) for i in range(2)]
    f0b = [es2.enter_context(nc.sbuf_tensor(f"f0b{i}", [128, 512], BF16)) for i in range(2)]
    f1b = [es2.enter_context(nc.sbuf_tensor(f"f1b{i}", [128, 512], BF16)) for i in range(2)]
    gateb = [es2.enter_context(nc.sbuf_tensor(f"gate{i}", [128, 512], BF16)) for i in range(2)]
    prodb = [es2.enter_context(nc.sbuf_tensor(f"prod{i}", [128, 512], BF16)) for i in range(2)]
    lbuf = [es2.enter_context(nc.sbuf_tensor(f"lbuf{i}", [128, D], F32)) for i in range(2)]
    lq = [es2.enter_context(nc.sbuf_tensor(f"lq{i}", [128, HID], E4)) for i in range(2)]
    ffnb = [es2.enter_context(nc.sbuf_tensor(f"ffnb{i}", [128, 512], BF16)) for i in range(2)]
    gsil = [es2.enter_context(nc.sbuf_tensor(f"gsil{i}", [128, 512], BF16)) for i in range(2)]
    pfa = [es2.enter_context(nc.psum_tensor(f"pfa{i}", [128, 512], F32)) for i in range(2)]
    pfb = [es2.enter_context(nc.psum_tensor(f"pfb{i}", [128, 512], F32)) for i in range(2)]

    r_g0buf = [Res(), Res(), Res()]
    r_g1buf = [Res(), Res(), Res()]
    r_g0q = [Res(), Res(), Res()]
    r_g1q = [Res(), Res(), Res()]
    r_acts_hc = [Res(), Res()]
    r_acts_sb = [Res(), Res()]
    r_f0b = [Res(), Res()]
    r_f1b = [Res(), Res()]
    r_gate = [Res(), Res()]
    r_prod = [Res(), Res()]
    r_lbuf = [Res(), Res()]
    r_lq = [Res(), Res()]
    r_ffnb = [Res(), Res()]
    r_gsil = [Res(), Res()]
    r_pfa = [Res(), Res()]
    r_pfb = [Res(), Res()]
    R["acts_dr"] = Res()

    if 2 in phases:
        q3 = q_t[:].rearrange("p (dc t) -> p dc t", dc=NDC)
        def a_load(hc):
            p = hc % 3
            b.dma("sy", g0buf[p][:], wg0_d[hc], writes=[r_g0buf[p]])
            b.dma("sy", g1buf[p][:], wg1_d[hc], writes=[r_g1buf[p]])

        def a_quant(hc):
            p = hc % 3
            b.op("act", lambda e, p=p: e.copy(out=g0q[p][:, 0:1024], in_=g0buf[p][:, 0:1024]),
                 reads=[r_g0buf[p]], writes=[r_g0q[p]])
            b.op("dve", lambda e, p=p: e.tensor_copy(
                out=g0q[p][:, 1024:2048], in_=g0buf[p][:, 1024:2048]),
                reads=[r_g0buf[p]], writes=[r_g0q[p]])
            b.op("act", lambda e, p=p: e.copy(out=g1q[p][:, 0:1024], in_=g1buf[p][:, 0:1024]),
                 reads=[r_g1buf[p]], writes=[r_g1q[p]])
            b.op("dve", lambda e, p=p: e.tensor_copy(
                out=g1q[p][:, 1024:2048], in_=g1buf[p][:, 1024:2048]),
                reads=[r_g1buf[p]], writes=[r_g1q[p]])

        a_load(0)
        a_load(1)
        a_quant(0)
        for hc in range(NHC):
            p = hc % 3
            if hc + 2 < NHC:
                a_load(hc + 2)
            if hc + 1 < NHC:
                a_quant(hc + 1)
            g0q3 = g0q[p][:].rearrange("p (dc m) -> p dc m", dc=NDC)
            g1q3 = g1q[p][:].rearrange("p (dc m) -> p dc m", dc=NDC)
            for tq in range(4):
                tcol = tq * 512
                ep = tq % 2
                builds = [
                    (lambda e, j=j, tcol=tcol, ep=ep, g0q3=g0q3, q3=q3: e.matmul(
                        out=pfa[ep][:], lhsT=g0q3[:, 2 * j:2 * j + 2, :],
                        rhs=q3[:, 2 * j:2 * j + 2, tcol:tcol + 512],
                        start=(j == 0), stop=(j == NDC // 2 - 1), perf_mode=DR))
                    for j in range(NDC // 2)]
                b.op_group("pe", builds, reads=[r_g0q[p], R["q"]], writes=[r_pfa[ep]])
                builds = [
                    (lambda e, j=j, tcol=tcol, ep=ep, g1q3=g1q3, q3=q3: e.matmul(
                        out=pfb[ep][:], lhsT=g1q3[:, 2 * j:2 * j + 2, :],
                        rhs=q3[:, 2 * j:2 * j + 2, tcol:tcol + 512],
                        start=(j == 0), stop=(j == NDC // 2 - 1), perf_mode=DR))
                    for j in range(NDC // 2)]
                b.op_group("pe", builds, reads=[r_g1q[p], R["q"]], writes=[r_pfb[ep]])
                b.op("dve", lambda e, ep=ep: e.tensor_copy(out=f1b[ep][:], in_=pfb[ep][:]),
                     reads=[r_pfb[ep]], writes=[r_f1b[ep]])
                b.op("act", lambda e, ep=ep: e.activation(
                    out=gateb[ep][:], in_=pfa[ep][:], func=AF.Gelu_apprx_tanh),
                    reads=[r_pfa[ep]], writes=[r_gate[ep]])
                b.op("dve", lambda e, ep=ep: e.tensor_mul(
                    out=prodb[ep][:], in0=gateb[ep][:], in1=f1b[ep][:]),
                    reads=[r_gate[ep], r_f1b[ep]], writes=[r_prod[ep]])
                b.op("act", lambda e, ep=ep, ap2=hc % 2, tcol=tcol: e.copy(
                    out=acts_hc[ap2][:, tcol:tcol + 512], in_=prodb[ep][:]),
                    reads=[r_prod[ep]], writes=[r_acts_hc[hc % 2]])
            for tq in range(4):
                b.dma("gp", acts_d[tq][:, hc * 512:(hc + 1) * 512],
                      acts_hc[hc % 2][:, tq * 512:(tq + 1) * 512],
                      reads=[r_acts_hc[hc % 2]], writes=[R["acts_dr"]])


        es_q.close()
        acts_sb = [es2.enter_context(nc.sbuf_tensor(f"acts_sb{i}", [128, NHC * 512], E4))
                   for i in range(2)]
        def b_feed(tq, dc):
            pl = dc % 2
            if tq == 0:
                for hq in range(4):
                    p = hq % 2
                    b.dma("sy" if hq < 2 else "gp", lbuf[p][:],
                          wlin_d[dc][:, hq * 2048:(hq + 1) * 2048], writes=[r_lbuf[p]])
                    eng = "act" if hq % 2 == 0 else "dve"
                    fn = ((lambda e, p=p, hq=hq, pl=pl: e.copy(
                        out=lq[pl][:, hq * 2048:(hq + 1) * 2048], in_=lbuf[p][:]))
                        if eng == "act" else
                        (lambda e, p=p, hq=hq, pl=pl: e.tensor_copy(
                            out=lq[pl][:, hq * 2048:(hq + 1) * 2048], in_=lbuf[p][:])))
                    b.op(eng, fn, reads=[r_lbuf[p]], writes=[r_lq[pl]])
                b.dma("gp", wlinq_d[dc], lq[pl][:], reads=[r_lq[pl]], writes=[R["wlinq_dr"]])
            else:
                b.dma("sy", lq[pl][:], wlinq_d[dc], reads=[R["wlinq_dr"]], writes=[r_lq[pl]])

        b.dma("sy", acts_sb[0][:], acts_d[0],
              reads=[R["acts_dr"], R["q"]], writes=[r_acts_sb[0]])
        b_feed(0, 0)
        for tq in range(4):
            tcol = tq * 512
            ab = tq % 2
            if tq + 1 < 4:
                b.dma("sy", acts_sb[1 - ab][:], acts_d[tq + 1],
                      reads=[R["acts_dr"]], writes=[r_acts_sb[1 - ab]])
            acts3 = acts_sb[ab][:].rearrange("p (hc t) -> p hc t", hc=NHC)
            for dc in range(NDC):
                pl = dc % 2
                if dc + 1 < NDC:
                    b_feed(tq, dc + 1)
                elif tq + 1 < 4:
                    b_feed(tq + 1, 0)
                ep = dc % 2
                lq3 = lq[pl][:].rearrange("p (hc m) -> p hc m", hc=NHC)
                builds = [
                    (lambda e, j=j, ep=ep, lq3=lq3, acts3=acts3: e.matmul(
                        out=pfa[ep][:], lhsT=lq3[:, 2 * j:2 * j + 2, :],
                        rhs=acts3[:, 2 * j:2 * j + 2, :],
                        start=(j == 0), stop=(j == NHC // 2 - 1), perf_mode=DR))
                    for j in range(NHC // 2)]
                b.op_group("pe", builds, reads=[r_lq[pl], r_acts_sb[ab]], writes=[r_pfa[ep]])
                b.op("act", lambda e, ep=ep: e.activation(
                    out=gsil[ep][:], in_=pfa[ep][:], func=AF.Silu),
                    reads=[r_pfa[ep]], writes=[r_gsil[ep]])
                b.dma("gp", g_d[dc][:, tcol:tcol + 512], gsil[ep][:],
                      reads=[r_gsil[ep]], writes=[R["g_dr"]])

    es2.close()
    es_q.close()
    b.barrier()

    es3 = ExitStack()
    xnf_sb = es3.enter_context(nc.sbuf_tensor("xnf_sb", [128, NDC * 512], F32))
    xnhs = es3.enter_context(nc.sbuf_tensor("xnhs", [128, NDC * 512], BF16))
    xnls = es3.enter_context(nc.sbuf_tensor("xnls", [128, NDC * 512], BF16))
    gs = es3.enter_context(nc.sbuf_tensor("gs", [128, NDC * 512], BF16))
    wbuf = [es3.enter_context(nc.sbuf_tensor(f"wbuf{i}", [128, D], F32)) for i in range(2)]
    wh = [es3.enter_context(nc.sbuf_tensor(f"wh{i}", [128, D], BF16)) for i in range(2)]
    wl = [es3.enter_context(nc.sbuf_tensor(f"wl{i}", [128, D], BF16)) for i in range(2)]
    u_t = [es3.enter_context(nc.sbuf_tensor(f"u_t{i}", [128, 512], F32)) for i in range(2)]
    ysh = es3.enter_context(nc.sbuf_tensor("ysh", [128, NDC * 512], BF16))
    ysl = es3.enter_context(nc.sbuf_tensor("ysl", [128, NDC * 512], BF16))
    dtv = es3.enter_context(nc.sbuf_tensor("dtv", [32, 512], F32))
    dth = es3.enter_context(nc.sbuf_tensor("dth", [32, 512], BF16))
    dtl = es3.enter_context(nc.sbuf_tensor("dtl", [32, 512], BF16))
    a_t = [es3.enter_context(nc.sbuf_tensor(f"a_t{i}", [128, 512], F32)) for i in range(2)]
    ys_t = [es3.enter_context(nc.sbuf_tensor(f"ys_t{i}", [128, 512], F32)) for i in range(2)]
    z_t = [es3.enter_context(nc.sbuf_tensor(f"z_t{i}", [128, 512], F32)) for i in range(2)]
    xbt = [es3.enter_context(nc.sbuf_tensor(f"xbt{i}", [128, 512], F32)) for i in range(4)]
    ot = [es3.enter_context(nc.sbuf_tensor(f"ot{i}", [128, 512], F32)) for i in range(4)]
    pacc = [es3.enter_context(nc.psum_tensor(f"pacc{i}", [128, 512], F32)) for i in range(2)]
    psb = [es3.enter_context(nc.psum_tensor(f"psb{i}", [128, 512], F32)) for i in range(2)]
    pstb = [es3.enter_context(nc.psum_tensor(f"pstb{i}", [128, 512], F32)) for i in range(4)]

    r_xnf, r_xnhs, r_xnls, r_gs = Res(), Res(), Res(), Res()
    r_u2 = [Res(), Res()]
    r_wbuf = [Res(), Res()]
    r_wh = [Res(), Res()]
    r_wl = [Res(), Res()]
    r_u = Res()
    r_ys = Res()
    r_dt = Res()
    r_a = [Res(), Res()]
    r_yst = [Res(), Res()]
    r_z = [Res(), Res()]
    r_xbt = [Res() for _ in range(4)]
    r_ot = [Res() for _ in range(4)]
    r_pacc = [Res(), Res()]
    r_psb = [Res(), Res()]
    r_pstb = [Res() for _ in range(4)]

    def feed_w(s, ec, which):
        pp = ec % 2
        hd = winh_d[ec] if which == 0 else wouth_d[ec]
        ld = winl_d[ec] if which == 0 else woutl_d[ec]
        rr = R["winhl_dr"] if which == 0 else R["wouthl_dr"]
        if s == 0:
            src = winT_d[ec] if which == 0 else woutT_d[ec]
            b.dma("sy", wbuf[pp][:], src, writes=[r_wbuf[pp]])
            b.op("dve", lambda e, pp=pp: e.tensor_copy(out=wh[pp][:], in_=wbuf[pp][:]),
                 reads=[r_wbuf[pp]], writes=[r_wh[pp]])
            b.op("dve", lambda e, pp=pp: e.tensor_sub(out=wl[pp][:], in0=wbuf[pp][:], in1=wh[pp][:]),
                 reads=[r_wbuf[pp], r_wh[pp]], writes=[r_wl[pp]])
            b.dma("gp", hd, wh[pp][:], reads=[r_wh[pp]], writes=[rr])
            b.dma("gp", ld, wl[pp][:], reads=[r_wl[pp]], writes=[rr])
        else:
            b.dma("sy", wh[pp][:], hd, reads=[rr], writes=[r_wh[pp]])
            b.dma("sy", wl[pp][:], ld, reads=[rr], writes=[r_wl[pp]])

    def mm3(pp, pdst, rdst, rhs_h, rhs_l, r_rh, r_rl):
        builds = []
        nk = 3 * NDC
        k = 0
        for dc in range(NDC):
            for (lhs, rhs) in ((wh[pp], rhs_h), (wh[pp], rhs_l), (wl[pp], rhs_h)):
                builds.append(lambda e, lhs=lhs, rhs=rhs, dc=dc, k=k: e.matmul(
                    out=pdst[:], lhsT=lhs[:, dc * 128:(dc + 1) * 128],
                    rhs=rhs[:, dc * 512:(dc + 1) * 512],
                    start=(k == 0), stop=(k == nk - 1)))
                k += 1
        b.op_group("pe", builds, reads=[r_wh[pp], r_wl[pp], r_rh, r_rl], writes=[rdst])

    def load_xn_slice(s):
        b.dma("sy", xnf_sb[:], xnf_d[:, :, s * 512:(s + 1) * 512].rearrange("dc p t -> p dc t"),
              reads=[R["xnf_dr"]], writes=[r_xnf])

    def split_xn_chunk(dc):
        b.op("dve", lambda e, dc=dc: e.tensor_copy(
            out=xnhs[:, dc * 512:(dc + 1) * 512], in_=xnf_sb[:, dc * 512:(dc + 1) * 512]),
            reads=[r_xnf], writes=[r_xnhs])
        b.op("dve", lambda e, dc=dc: e.tensor_sub(
            out=xnls[:, dc * 512:(dc + 1) * 512], in0=xnf_sb[:, dc * 512:(dc + 1) * 512],
            in1=xnhs[:, dc * 512:(dc + 1) * 512]),
            reads=[r_xnf, r_xnhs], writes=[r_xnls])

    def load_g_slice(s):
        b.dma("gp", gs[:], g_d[:, :, s * 512:(s + 1) * 512].rearrange("dc p t -> p dc t"),
              reads=[R["g_dr"]], writes=[r_gs])

    if 3 in phases:
        load_xn_slice(0)
        for dc in range(NDC):
            split_xn_chunk(dc)
        load_g_slice(0)
    for s in range(NS if 3 in phases else 0):
        builds = []
        nk = 3 * NDC
        k = 0
        for dc in range(NDC):
            for (lhs, rhs) in ((wct_h, xnhs), (wct_h, xnls), (wct_l, xnhs)):
                builds.append(lambda e, lhs=lhs, rhs=rhs, dc=dc, k=k: e.matmul(
                    out=psb[0][0:32, :], lhsT=lhs[:, dc * 32:(dc + 1) * 32],
                    rhs=rhs[:, dc * 512:(dc + 1) * 512],
                    start=(k == 0), stop=(k == nk - 1)))
                k += 1
        b.op_group("pe", builds, reads=[R["wct_h"], R["wct_l"], r_xnhs, r_xnls],
                   writes=[r_psb[0]])
        b.op("act", lambda e: e.activation(out=dtv[:], in_=psb[0][0:32, :], func=AF.Exp,
                                           bias=bdt_t[:]),
             reads=[r_psb[0], R["bdt_t"]], writes=[r_dt])
        b.op("act", lambda e: e.activation(out=dtv[:], in_=dtv[:], func=AF.Ln, bias=onet[:]),
             reads=[r_dt, R["onet"]], writes=[r_dt])
        b.op("dve", lambda e: e.tensor_copy(out=dth[:], in_=dtv[:]),
             reads=[r_dt], writes=[r_dt])
        b.op("dve", lambda e: e.tensor_sub(out=dtl[:], in0=dtv[:], in1=dth[:]),
             reads=[r_dt], writes=[r_dt])

        def scan_chain(ct):
            pp = ct % 2
            b.op_group("pe", [
                lambda e, ct=ct, pp=pp: e.matmul(
                    out=psb[pp][:], lhsT=bsel_t[:, ct * 128:(ct + 1) * 128], rhs=dth[:],
                    start=True, stop=False),
                lambda e, ct=ct, pp=pp: e.matmul(
                    out=psb[pp][:], lhsT=bsel_t[:, ct * 128:(ct + 1) * 128], rhs=dtl[:],
                    start=False, stop=True)],
                reads=[R["bsel_t"], r_dt], writes=[r_psb[pp]])
            b.op("act", lambda e, ct=ct, pp=pp: e.activation(
                out=a_t[pp][:], in_=psb[pp][:], func=AF.Exp, scale=negA[:, ct:ct + 1]),
                reads=[r_psb[pp], R["negA"]], writes=[r_a[pp]])
            if s == 0:
                b.op("dve", lambda e, ct=ct, pp=pp: e.tensor_tensor_scan(
                    out=ys_t[pp][:], data0=a_t[pp][:], data1=u_t[pp][:],
                    initial=0.0, op0=ALU.mult, op1=ALU.add),
                    reads=[r_a[pp], r_u2[pp]], writes=[r_yst[pp]])
            else:
                b.op("dve", lambda e, ct=ct, pp=pp: e.tensor_tensor_scan(
                    out=ys_t[pp][:], data0=a_t[pp][:], data1=u_t[pp][:],
                    initial=state[:, ct:ct + 1], op0=ALU.mult, op1=ALU.add),
                    reads=[r_a[pp], r_u2[pp], R["state"]], writes=[r_yst[pp]])
            b.op("dve", lambda e, ct=ct, pp=pp: e.tensor_copy(
                out=state[:, ct:ct + 1], in_=ys_t[pp][:, 511:512]),
                reads=[r_yst[pp]], writes=[R["state"]])
            b.op("act", lambda e, ct=ct, pp=pp: e.copy(
                out=ysh[:, ct * 512:(ct + 1) * 512], in_=ys_t[pp][:]),
                reads=[r_yst[pp]], writes=[r_ys])
            b.op("dve", lambda e, ct=ct, pp=pp: e.tensor_sub(
                out=ysl[:, ct * 512:(ct + 1) * 512], in0=ys_t[pp][:],
                in1=ysh[:, ct * 512:(ct + 1) * 512]),
                reads=[r_yst[pp], r_ys], writes=[r_ys])

        feed_w(s, 0, 0)
        for ec in range(NDC):
            if ec + 1 < NDC:
                feed_w(s, ec + 1, 0)
            else:
                feed_w(s, 0, 1)
            pp = ec % 2
            mm3(pp, pacc[pp], r_pacc[pp], xnhs, xnls, r_xnhs, r_xnls)
            b.op("dve", lambda e, ec=ec, pp=pp: e.tensor_copy(
                out=u_t[pp][:], in_=pacc[pp][:]),
                reads=[r_pacc[pp]], writes=[r_u2[pp]])
            scan_chain(ec)
        if s + 1 < NS:
            load_xn_slice(s + 1)

        for eg in range(4):
            for tb4 in range(4):
                row = s * 512 + tb4 * 128
                b.dma("gp", xbt[tb4][:], x_d[row:row + 128, eg * 512:(eg + 1) * 512],
                      writes=[r_xbt[tb4]])
            for e4i in range(4):
                ec = eg * 4 + e4i
                if ec + 1 < NDC:
                    feed_w(s, ec + 1, 1)
                pp = ec % 2
                mm3(pp, pacc[pp], r_pacc[pp], ysh, ysl, r_ys, r_ys)
                if s + 1 < NS:
                    split_xn_chunk(ec)
                b.op("dve", lambda e, ec=ec, pp=pp: e.tensor_tensor(
                    out=z_t[pp][:], in0=pacc[pp][:], in1=gs[:, ec * 512:(ec + 1) * 512],
                    op=ALU.mult),
                    reads=[r_pacc[pp], r_gs], writes=[r_z[pp]])
                builds = []
                for tb4 in range(4):
                    builds.append(lambda e, tb4=tb4, e4i=e4i, pp=pp: e.matmul(
                        out=pstb[tb4][:, e4i * 128:(e4i + 1) * 128],
                        lhsT=z_t[pp][:, tb4 * 128:(tb4 + 1) * 128], rhs=identb[:],
                        is_transpose=True, start=(e4i == 0), stop=(e4i == 3)))
                for tb4, bd in enumerate(builds):
                    b.op("pe", bd, reads=[r_z[pp], R["identb"]], writes=[r_pstb[tb4]])
            for tb4 in range(4):
                row = s * 512 + tb4 * 128
                b.op("dve", lambda e, tb4=tb4: e.tensor_add(
                    out=ot[tb4][:], in0=pstb[tb4][:], in1=xbt[tb4][:]),
                    reads=[r_pstb[tb4], r_xbt[tb4]], writes=[r_ot[tb4]])
                b.dma("gp", out_d[row:row + 128, eg * 512:(eg + 1) * 512], ot[tb4][:],
                      reads=[r_ot[tb4]], writes=[R["out_dr"]])
        if s + 1 < NS:
            load_g_slice(s + 1)

    es3.close()

    for _i in range(NQ):
        b._wait("gpsimd", f"gp{_i}", b.cnt[f"gp{_i}"])
        b._wait("sync", f"sy{_i}", b.cnt[f"sy{_i}"])

    with nc.Block() as block:
        @block.tensor
        def _(e):
            for f in b.cmds["tensor"]:
                f(e)

        @block.vector
        def _(e):
            for f in b.cmds["vector"]:
                f(e)

        @block.scalar
        def _(e):
            for f in b.cmds["scalar"]:
                f(e)

        @block.sync
        def _(e):
            for f in b.cmds["sync"]:
                f(e)

        @block.gpsimd
        def _(e):
            for f in b.cmds["gpsimd"]:
                f(e)

    es_g2.close()
    es_glob.close()
    return nc


def prepare_inputs(L0_attn_x, L0_attn_W_in, L0_attn_W_out, L0_attn_log_A, L0_attn_W_dt,
                   L0_attn_b_dt, L0_ffn_w_gating, L0_ffn_w_linear):
    f32 = np.float32
    x = np.ascontiguousarray(L0_attn_x, dtype=f32)

    def blk_cols(w):
        K, M = w.shape
        nm = M // 128
        nk = K // 128
        out = np.empty((nm, 128, K), dtype=f32)
        for mc in range(nm):
            out[mc] = np.ascontiguousarray(
                w[:, mc * 128:(mc + 1) * 128].reshape(nk, 128, 128).transpose(1, 0, 2)
                .reshape(128, K))
        return out

    WinT = np.ascontiguousarray(L0_attn_W_in.T, dtype=f32)
    WoutT = np.ascontiguousarray(L0_attn_W_out.T, dtype=f32)
    winT = blk_cols(WinT)
    woutT = blk_cols(WoutT)
    winraw = np.ascontiguousarray(L0_attn_W_in, dtype=f32)
    wg0 = blk_cols(np.ascontiguousarray(L0_ffn_w_gating[0], dtype=f32))
    wg1 = blk_cols(np.ascontiguousarray(L0_ffn_w_gating[1], dtype=f32))
    wlin = blk_cols(np.ascontiguousarray(L0_ffn_w_linear, dtype=f32))

    WD = np.ascontiguousarray(L0_attn_W_dt.T, dtype=f32)
    wdtT = np.ascontiguousarray(WD.reshape(16, 128, 32).transpose(1, 0, 2).reshape(128, 512))
    bdt = np.ascontiguousarray(L0_attn_b_dt.reshape(32, 1), dtype=f32)
    alog = np.ascontiguousarray(
        L0_attn_log_A.reshape(16, 2, 64).transpose(1, 2, 0).reshape(128, 16), dtype=f32)

    bsel = np.zeros((32, 16 * 128), dtype=ml_dtypes.bfloat16)
    for ct in range(16):
        for p in range(128):
            bsel[2 * ct + p // 64, ct * 128 + p] = 1.0
    ident = np.eye(128, dtype=f32)

    shared = dict(winT=winT, woutT=woutT, winraw=winraw, wdtT=wdtT, bdt=bdt, alog=alog,
                  bsel=bsel, ident=ident, wg0=wg0, wg1=wg1, wlin=wlin)
    return x, shared


_NC_CACHE = []


def kernel(L0_attn_x, L0_attn_W_in, L0_attn_W_out, L0_attn_log_A, L0_attn_W_dt,
           L0_attn_b_dt, L0_ffn_w_gating, L0_ffn_w_linear):
    x, shared = prepare_inputs(L0_attn_x, L0_attn_W_in, L0_attn_W_out, L0_attn_log_A,
                               L0_attn_W_dt, L0_attn_b_dt, L0_ffn_w_gating, L0_ffn_w_linear)
    if not _NC_CACHE:
        _NC_CACHE.append(build_nc())
    nc = _NC_CACHE[0]
    in_maps = [dict(shared, x=np.ascontiguousarray(x[i])) for i in range(8)]
    res = run_bass_kernel_spmd(nc, in_maps, core_ids=list(range(8)))
    return np.stack([np.asarray(r["out"]) for r in res.results], axis=0)


# version 15
# speedup vs baseline: 1.0247x; 1.0247x over previous
import os
import numpy as np
import ml_dtypes
from contextlib import ExitStack

import concourse.bass as bass
import concourse.mybir as mybir
from concourse.bass_utils import run_bass_kernel_spmd

F32 = mybir.dt.float32
BF16 = mybir.dt.bfloat16
E4 = mybir.dt.float8e4
AF = mybir.ActivationFunctionType
ALU = mybir.AluOpType
DR = mybir.MatmulPerfMode.DoubleRow

T = 2048
D = 2048
HID = 8192
NDC = D // 128
NHC = HID // 128
NS = 4
NTH = 2
NTBQ = 4

NQ = 4
ENG_OF = {"pe": "tensor", "dve": "vector", "act": "scalar"}
for _i in range(NQ):
    ENG_OF[f"sy{_i}"] = "sync"
    ENG_OF[f"gp{_i}"] = "gpsimd"
ALL_SEMS = list(ENG_OF.keys())


class Res:
    __slots__ = ("w", "r")

    def __init__(self):
        self.w = {}
        self.r = {}


class Builder:
    def __init__(self):
        self.cmds = {e: [] for e in ("tensor", "vector", "scalar", "sync", "gpsimd")}
        self.cnt = {s: 0 for s in ALL_SEMS + ["bar"]}
        self.sem = {}
        self.waited = {e: {} for e in self.cmds}
        self.nbar = 0
        self.rr = {"sy": 0, "gp": 0}

    def _wait(self, eng, sem, val):
        if val <= 0:
            return
        if self.waited[eng].get(sem, 0) >= val:
            return
        self.waited[eng][sem] = val
        s = self.sem[sem]
        self.cmds[eng].append(lambda e, s=s, v=val: e.wait_ge(s, v))

    def _edges(self, eng, reads, writes):
        for r in reads:
            for sm, v in r.w.items():
                self._wait(eng, sm, v)
        for r in writes:
            for sm, v in r.r.items():
                self._wait(eng, sm, v)
            for sm, v in r.w.items():
                self._wait(eng, sm, v)

    def _record(self, sem, v, reads, writes):
        for r in reads:
            r.r[sem] = v
        for r in writes:
            r.w = {sem: v}
            r.r = {}

    def op(self, sem, build, reads=(), writes=()):
        eng = ENG_OF[sem]
        self._edges(eng, reads, writes)
        self.cnt[sem] += 1
        v = self.cnt[sem]
        s = self.sem[sem]
        self.cmds[eng].append(lambda e, b=build, s=s: b(e).then_inc(s, 1))
        self._record(sem, v, reads, writes)
        return v

    def op_group(self, sem, builds, reads=(), writes=()):
        eng = ENG_OF[sem]
        self._edges(eng, reads, writes)
        self.cnt[sem] += 1
        v = self.cnt[sem]
        s = self.sem[sem]
        n = len(builds)
        for i, bd in enumerate(builds):
            if i == n - 1:
                self.cmds[eng].append(lambda e, b=bd, s=s: b(e).then_inc(s, 1))
            else:
                self.cmds[eng].append(lambda e, b=bd: b(e))
        self._record(sem, v, reads, writes)
        return v

    def dma(self, q, out_ap, in_ap, reads=(), writes=()):
        sem = f"{q}{self.rr[q]}"
        self.rr[q] = (self.rr[q] + 1) % NQ
        eng = ENG_OF[sem]
        self._wait(eng, sem, self.cnt[sem])
        self._edges(eng, reads, writes)
        self.cnt[sem] += 16
        v = self.cnt[sem]
        s = self.sem[sem]
        self.cmds[eng].append(
            lambda e, o=out_ap, i=in_ap, s=s: e.dma_start(out=o, in_=i).then_inc(s, 16))
        self._record(sem, v, reads, writes)
        return v

    def barrier(self):
        self.nbar += 1
        tgt = 5 * self.nbar
        bar = self.sem["bar"]
        for eng in self.cmds:
            for sm in ALL_SEMS:
                if ENG_OF[sm] == eng:
                    self._wait(eng, sm, self.cnt[sm])
            self.cmds[eng].append(lambda e, b=bar: e.sem_inc(b, 1))
            self._wait(eng, "bar", tgt)


def build_nc(phases=(0, 1, 2, 3)):
    nc = bass.Bass("TRN2", target_bir_lowering=False)

    x_d = nc.dram_tensor("x", [T, D], F32, kind="ExternalInput").ap()
    winT_d = nc.dram_tensor("winT", [NDC, 128, D], F32, kind="ExternalInput").ap()
    woutT_d = nc.dram_tensor("woutT", [NDC, 128, D], F32, kind="ExternalInput").ap()
    winraw_d = nc.dram_tensor("winraw", [D, D], F32, kind="ExternalInput").ap()
    wdtT_d = nc.dram_tensor("wdtT", [128, NDC * 32], F32, kind="ExternalInput").ap()
    bdt_d = nc.dram_tensor("bdt", [32, 1], F32, kind="ExternalInput").ap()
    alog_d = nc.dram_tensor("alog", [128, 16], F32, kind="ExternalInput").ap()
    bsel_d = nc.dram_tensor("bsel", [32, 16 * 128], BF16, kind="ExternalInput").ap()
    ident_d = nc.dram_tensor("ident", [128, 128], F32, kind="ExternalInput").ap()
    wg0_d = nc.dram_tensor("wg0", [NHC, 128, D], F32, kind="ExternalInput").ap()
    wg1_d = nc.dram_tensor("wg1", [NHC, 128, D], F32, kind="ExternalInput").ap()
    wlin_d = nc.dram_tensor("wlin", [NDC, 128, HID], F32, kind="ExternalInput").ap()
    out_d = nc.dram_tensor("out", [T, D], F32, kind="ExternalOutput").ap()
    xnh_d = nc.dram_tensor("xnh_s", [NDC, 128, T], BF16, kind="Internal").ap()
    xnl_d = nc.dram_tensor("xnl_s", [NDC, 128, T], BF16, kind="Internal").ap()
    g_d = nc.dram_tensor("g_sc", [NDC, 128, T], BF16, kind="Internal").ap()
    acts_d = nc.dram_tensor("acts_s", [4, 128, NHC * 512], E4, kind="Internal").ap()
    wlinq_d = nc.dram_tensor("wlinq_s", [NDC, 128, HID], E4, kind="Internal").ap()
    winh_d = nc.dram_tensor("winh_s", [NDC, 128, D], BF16, kind="Internal").ap()
    winl_d = nc.dram_tensor("winl_s", [NDC, 128, D], BF16, kind="Internal").ap()
    wouth_d = nc.dram_tensor("wouth_s", [NDC, 128, D], BF16, kind="Internal").ap()
    woutl_d = nc.dram_tensor("woutl_s", [NDC, 128, D], BF16, kind="Internal").ap()

    b = Builder()
    es_glob = ExitStack()
    for sname in ALL_SEMS + ["bar"]:
        b.sem[sname] = es_glob.enter_context(nc.semaphore("S_" + sname))

    es_g2 = ExitStack()
    identb = es_g2.enter_context(nc.sbuf_tensor("identb", [128, 128], F32))
    wdt_h = es_g2.enter_context(nc.sbuf_tensor("wdt_h", [128, NDC * 32], BF16))
    wdt_l = es_g2.enter_context(nc.sbuf_tensor("wdt_l", [128, NDC * 32], BF16))
    wct_h = es_g2.enter_context(nc.sbuf_tensor("wct_h", [128, NDC * 32], BF16))
    wct_l = es_g2.enter_context(nc.sbuf_tensor("wct_l", [128, NDC * 32], BF16))
    bdt_t = es_g2.enter_context(nc.sbuf_tensor("bdt_t", [32, 1], F32))
    negA = es_g2.enter_context(nc.sbuf_tensor("negA", [128, 16], F32))
    bsel_t = es_g2.enter_context(nc.sbuf_tensor("bsel_t", [32, 16 * 128], BF16))
    epst = es_g2.enter_context(nc.sbuf_tensor("epst", [128, 1], F32))
    onet = es_g2.enter_context(nc.sbuf_tensor("onet", [32, 1], F32))
    state = es_g2.enter_context(nc.sbuf_tensor("state", [128, 16], F32))

    R = {}
    for nm in ("identb", "wdt_h", "wdt_l", "wct_h", "wct_l", "bdt_t", "negA",
               "bsel_t", "epst", "onet", "state", "q", "acts", "xnh_dr", "xnl_dr",
               "g_dr", "out_dr", "wgq_dr", "wlinq_dr", "winhl_dr", "wouthl_dr"):
        R[nm] = Res()

    b.dma("sy", identb[:], ident_d, writes=[R["identb"]])
    b.dma("sy", bdt_t[:], bdt_d, writes=[R["bdt_t"]])
    b.dma("sy", negA[:], alog_d, writes=[R["negA"]])
    b.dma("sy", bsel_t[:], bsel_d, writes=[R["bsel_t"]])
    b.op("dve", lambda e: e.memset(epst[:], 1e-5), writes=[R["epst"]])
    b.op("dve", lambda e: e.memset(onet[:], 1.0), writes=[R["onet"]])
    b.op("act", lambda e: e.activation(out=negA[:], in_=negA[:], func=AF.Exp),
         reads=[R["negA"]], writes=[R["negA"]])
    b.op("dve", lambda e: e.tensor_scalar_mul(out=negA[:], in0=negA[:], scalar1=-1.0),
         reads=[R["negA"]], writes=[R["negA"]])

    if 0 in phases:
        es0 = ExitStack()
        wdt_f = es0.enter_context(nc.sbuf_tensor("wdt_f", [128, NDC * 32], F32))
        w0buf = [es0.enter_context(nc.sbuf_tensor(f"w0b{i}", [128, D], F32)) for i in range(2)]
        w0h = [es0.enter_context(nc.sbuf_tensor(f"w0h{i}", [128, D], BF16)) for i in range(2)]
        w0l = [es0.enter_context(nc.sbuf_tensor(f"w0l{i}", [128, D], BF16)) for i in range(2)]
        wc_f = es0.enter_context(nc.sbuf_tensor("wc_f", [32, D], F32))
        wct_f = es0.enter_context(nc.sbuf_tensor("wct_f", [128, NDC * 32], F32))
        pwc = [es0.enter_context(nc.psum_tensor(f"pwc{i}", [32, 512], F32)) for i in range(4)]
        pwt = es0.enter_context(nc.psum_tensor("pwt", [128, 512], F32))
        r_wdtf = Res()
        r_w0buf = [Res(), Res()]
        r_w0h = [Res(), Res()]
        r_w0l = [Res(), Res()]
        r_wcf = Res()
        r_wctf = Res()
        r_pwc = [Res() for _ in range(4)]
        r_pwt = Res()

        b.dma("sy", wdt_f[:], wdtT_d, writes=[r_wdtf])
        b.op("dve", lambda e: e.tensor_copy(out=wdt_h[:], in_=wdt_f[:]),
             reads=[r_wdtf], writes=[R["wdt_h"]])
        b.op("dve", lambda e: e.tensor_sub(out=wdt_l[:], in0=wdt_f[:], in1=wdt_h[:]),
             reads=[r_wdtf, R["wdt_h"]], writes=[R["wdt_l"]])
        for ns in range(4):
            builds = []
            k = 0
            nk = 3 * NDC
            for ec in range(NDC):
                p = ec % 2
                for (lhs, rhs) in ((wdt_h, w0h[p]), (wdt_h, w0l[p]), (wdt_l, w0h[p])):
                    builds.append(lambda e, lhs=lhs, rhs=rhs, ec=ec, ns=ns, k=k: e.matmul(
                        out=pwc[ns][:], lhsT=lhs[:, ec * 32:(ec + 1) * 32],
                        rhs=rhs[:, ns * 512:(ns + 1) * 512],
                        start=(k == 0), stop=(k == nk - 1)))
                    k += 1
            if ns == 0:
                wc_builds = [builds]
            else:
                wc_builds.append(builds)
        for ec in range(NDC):
            p = ec % 2
            b.dma("sy", w0buf[p][:], winraw_d[ec * 128:(ec + 1) * 128, :], writes=[r_w0buf[p]])
            b.op("dve", lambda e, p=p: e.tensor_copy(out=w0h[p][:], in_=w0buf[p][:]),
                 reads=[r_w0buf[p]], writes=[r_w0h[p]])
            b.op("dve", lambda e, p=p: e.tensor_sub(out=w0l[p][:], in0=w0buf[p][:], in1=w0h[p][:]),
                 reads=[r_w0buf[p], r_w0h[p]], writes=[r_w0l[p]])
            for ns in range(4):
                b.op_group("pe", wc_builds[ns][ec * 3:(ec + 1) * 3],
                           reads=[R["wdt_h"], R["wdt_l"], r_w0h[p], r_w0l[p]],
                           writes=[r_pwc[ns]])
        for ns in range(4):
            b.op("dve", lambda e, ns=ns: e.tensor_copy(out=wc_f[:, ns * 512:(ns + 1) * 512],
                                                       in_=pwc[ns][:]),
                 reads=[r_pwc[ns]], writes=[r_wcf])
        builds = []
        for ec in range(NDC):
            builds.append(lambda e, ec=ec: e.matmul(
                out=pwt[:, ec * 32:(ec + 1) * 32], lhsT=wc_f[:, ec * 128:(ec + 1) * 128],
                rhs=identb[0:32, 0:32], is_transpose=True,
                start=(ec == 0), stop=(ec == NDC - 1)))
        b.op_group("pe", builds, reads=[r_wcf, R["identb"]], writes=[r_pwt])
        b.op("dve", lambda e: e.tensor_copy(out=wct_f[:], in_=pwt[:]),
             reads=[r_pwt], writes=[r_wctf])
        b.op("dve", lambda e: e.tensor_copy(out=wct_h[:], in_=wct_f[:]),
             reads=[r_wctf], writes=[R["wct_h"]])
        b.op("dve", lambda e: e.tensor_sub(out=wct_l[:], in0=wct_f[:], in1=wct_h[:]),
             reads=[r_wctf, R["wct_h"]], writes=[R["wct_l"]])
        es0.close()

    b.barrier()

    es_q = ExitStack()
    q_t = es_q.enter_context(nc.sbuf_tensor("q_t", [128, NDC * T], E4, side="right"))

    es1 = ExitStack()
    xt = [es1.enter_context(nc.sbuf_tensor(f"xt{i}", [128, D], F32)) for i in range(4)]
    xn_t = [es1.enter_context(nc.sbuf_tensor(f"xn{i}", [128, D], F32)) for i in range(8)]
    bns = es1.enter_context(nc.sbuf_tensor("bns", [128, 8 * 6 * 4], F32))
    mv = es1.enter_context(nc.sbuf_tensor("mv", [128, 16], F32))
    rstd = es1.enter_context(nc.sbuf_tensor("rstd", [128, 8], F32))
    nmr = es1.enter_context(nc.sbuf_tensor("nmr", [128, 8], F32))
    xnh_st = [es1.enter_context(nc.sbuf_tensor(f"xnh_st{i}", [128, NDC * 512], BF16)) for i in range(2)]
    xnl_st = [es1.enter_context(nc.sbuf_tensor(f"xnl_st{i}", [128, NDC * 512], BF16)) for i in range(2)]
    ps1 = [es1.enter_context(nc.psum_tensor(f"ps1_{i}", [128, 512], F32)) for i in range(4)]

    r_xt = [Res() for _ in range(4)]
    r_xn = [Res() for _ in range(8)]
    r_stats = [Res() for _ in range(8)]
    r_xnst = [Res(), Res()]
    r_ps1 = [Res() for _ in range(4)]

    for tbq in range(NTBQ if 1 in phases else 0):
        tp_ = tbq % 2
        for tb4 in range(4):
            tb = tbq * 4 + tb4
            p = tb % 4
            sl = tb4 + 4 * tp_
            b.dma("sy", xt[p][:], x_d[tb * 128:(tb + 1) * 128, :], writes=[r_xt[p]])
            for g in range(4):
                b.op("dve", lambda e, p=p, sl=sl, g=g: e.bn_stats(
                    out=bns[:, (sl * 4 + g) * 6:(sl * 4 + g + 1) * 6],
                    in_=xt[p][:, g * 512:(g + 1) * 512]),
                    reads=[r_xt[p]], writes=[r_stats[sl]])
            b.op("dve", lambda e, sl=sl: e.bn_aggr(
                out=mv[:, sl * 2:sl * 2 + 2], in_=bns[:, sl * 24:(sl + 1) * 24]),
                reads=[r_stats[sl]], writes=[r_stats[sl]])
            b.op("act", lambda e, sl=sl: e.activation(
                out=rstd[:, sl:sl + 1], in_=mv[:, sl * 2 + 1:sl * 2 + 2],
                func=AF.Sqrt, bias=epst[:]),
                reads=[r_stats[sl], R["epst"]], writes=[r_stats[sl]])
            b.op("dve", lambda e, sl=sl: e.reciprocal(
                out=rstd[:, sl:sl + 1], in_=rstd[:, sl:sl + 1]),
                reads=[r_stats[sl]], writes=[r_stats[sl]])
            b.op("dve", lambda e, sl=sl: e.tensor_mul(
                out=nmr[:, sl:sl + 1], in0=mv[:, sl * 2:sl * 2 + 1],
                in1=rstd[:, sl:sl + 1]),
                reads=[r_stats[sl]], writes=[r_stats[sl]])
            b.op("dve", lambda e, sl=sl: e.tensor_scalar_mul(
                out=nmr[:, sl:sl + 1], in0=nmr[:, sl:sl + 1], scalar1=-1.0),
                reads=[r_stats[sl]], writes=[r_stats[sl]])
            b.op("act", lambda e, p=p, sl=sl: e.activation(
                out=xn_t[sl][:], in_=xt[p][:], func=AF.Identity,
                scale=rstd[:, sl:sl + 1], bias=nmr[:, sl:sl + 1]),
                reads=[r_xt[p], r_stats[sl]], writes=[r_xn[sl]])
        for dc in range(NDC):
            pp = dc % 2 + 2 * tp_
            builds = []
            for tb4 in range(4):
                builds.append(lambda e, dc=dc, tb4=tb4, pp=pp, tp_=tp_: e.matmul(
                    out=ps1[pp][:, tb4 * 128:(tb4 + 1) * 128],
                    lhsT=xn_t[tb4 + 4 * tp_][:, dc * 128:(dc + 1) * 128],
                    rhs=identb[:], is_transpose=True,
                    start=(tb4 == 0), stop=(tb4 == 3)))
            b.op_group("pe", builds,
                       reads=[r_xn[4 * tp_], r_xn[1 + 4 * tp_], r_xn[2 + 4 * tp_],
                              r_xn[3 + 4 * tp_], R["identb"]],
                       writes=[r_ps1[pp]])
            b.op("act", lambda e, dc=dc, pp=pp, tp_=tp_: e.copy(
                out=xnh_st[tp_][:, dc * 512:(dc + 1) * 512], in_=ps1[pp][:]),
                reads=[r_ps1[pp]], writes=[r_xnst[tp_]])
            b.op("dve", lambda e, dc=dc, pp=pp, tp_=tp_: e.tensor_sub(
                out=xnl_st[tp_][:, dc * 512:(dc + 1) * 512], in0=ps1[pp][:],
                in1=xnh_st[tp_][:, dc * 512:(dc + 1) * 512]),
                reads=[r_ps1[pp], r_xnst[tp_]], writes=[r_xnst[tp_]])
            b.op("dve", lambda e, dc=dc, tbq=tbq, pp=pp: e.tensor_copy(
                out=q_t[:, dc * T + tbq * 512: dc * T + (tbq + 1) * 512], in_=ps1[pp][:]),
                reads=[r_ps1[pp]], writes=[R["q"]])
        b.dma("gp", xnh_d[:, :, tbq * 512:(tbq + 1) * 512].rearrange("dc p t -> p dc t"),
              xnh_st[tp_][:], reads=[r_xnst[tp_]], writes=[R["xnh_dr"]])
        b.dma("gp", xnl_d[:, :, tbq * 512:(tbq + 1) * 512].rearrange("dc p t -> p dc t"),
              xnl_st[tp_][:], reads=[r_xnst[tp_]], writes=[R["xnl_dr"]])

    es1.close()
    b.barrier()

    es2 = ExitStack()
    g0buf = [es2.enter_context(nc.sbuf_tensor(f"g0b{i}", [128, D], F32)) for i in range(2)]
    g1buf = [es2.enter_context(nc.sbuf_tensor(f"g1b{i}", [128, D], F32)) for i in range(2)]
    g0q = [es2.enter_context(nc.sbuf_tensor(f"g0q{i}", [128, D], E4)) for i in range(2)]
    g1q = [es2.enter_context(nc.sbuf_tensor(f"g1q{i}", [128, D], E4)) for i in range(2)]
    acts_hc = [es2.enter_context(nc.sbuf_tensor(f"acts{i}", [128, T], E4)) for i in range(2)]
    f0b = [es2.enter_context(nc.sbuf_tensor(f"f0b{i}", [128, 512], BF16)) for i in range(2)]
    f1b = [es2.enter_context(nc.sbuf_tensor(f"f1b{i}", [128, 512], BF16)) for i in range(2)]
    gateb = [es2.enter_context(nc.sbuf_tensor(f"gate{i}", [128, 512], BF16)) for i in range(2)]
    prodb = [es2.enter_context(nc.sbuf_tensor(f"prod{i}", [128, 512], BF16)) for i in range(2)]
    lbuf = [es2.enter_context(nc.sbuf_tensor(f"lbuf{i}", [128, D], F32)) for i in range(2)]
    lq = [es2.enter_context(nc.sbuf_tensor(f"lq{i}", [128, HID], E4)) for i in range(2)]
    ffnb = [es2.enter_context(nc.sbuf_tensor(f"ffnb{i}", [128, 512], BF16)) for i in range(2)]
    gsil = [es2.enter_context(nc.sbuf_tensor(f"gsil{i}", [128, 512], BF16)) for i in range(2)]
    pfa = [es2.enter_context(nc.psum_tensor(f"pfa{i}", [128, 512], F32)) for i in range(2)]
    pfb = [es2.enter_context(nc.psum_tensor(f"pfb{i}", [128, 512], F32)) for i in range(2)]

    r_g0buf = [Res(), Res()]
    r_g1buf = [Res(), Res()]
    r_g0q = [Res(), Res()]
    r_g1q = [Res(), Res()]
    r_acts_hc = [Res(), Res()]
    r_acts_sb = [Res(), Res()]
    r_f0b = [Res(), Res()]
    r_f1b = [Res(), Res()]
    r_gate = [Res(), Res()]
    r_prod = [Res(), Res()]
    r_lbuf = [Res(), Res()]
    r_lq = [Res(), Res()]
    r_ffnb = [Res(), Res()]
    r_gsil = [Res(), Res()]
    r_pfa = [Res(), Res()]
    r_pfb = [Res(), Res()]
    R["acts_dr"] = Res()

    if 2 in phases:
        q3 = q_t[:].rearrange("p (dc t) -> p dc t", dc=NDC)
        b.dma("sy", g0buf[0][:], wg0_d[0], writes=[r_g0buf[0]])
        b.dma("sy", g1buf[0][:], wg1_d[0], writes=[r_g1buf[0]])
        b.op("act", lambda e: e.copy(out=g0q[0][:, 0:1024], in_=g0buf[0][:, 0:1024]),
             reads=[r_g0buf[0]], writes=[r_g0q[0]])
        b.op("dve", lambda e: e.tensor_copy(out=g0q[0][:, 1024:2048], in_=g0buf[0][:, 1024:2048]),
             reads=[r_g0buf[0]], writes=[r_g0q[0]])
        b.op("act", lambda e: e.copy(out=g1q[0][:, 0:1024], in_=g1buf[0][:, 0:1024]),
             reads=[r_g1buf[0]], writes=[r_g1q[0]])
        b.op("dve", lambda e: e.tensor_copy(out=g1q[0][:, 1024:2048], in_=g1buf[0][:, 1024:2048]),
             reads=[r_g1buf[0]], writes=[r_g1q[0]])
        for hc in range(NHC):
            p = hc % 2
            if hc + 1 < NHC:
                b.dma("sy", g0buf[1 - p][:], wg0_d[hc + 1], writes=[r_g0buf[1 - p]])
                b.dma("sy", g1buf[1 - p][:], wg1_d[hc + 1], writes=[r_g1buf[1 - p]])
            if hc + 1 < NHC:
                b.op("act", lambda e, p=p: e.copy(
                    out=g0q[1 - p][:, 0:1024], in_=g0buf[1 - p][:, 0:1024]),
                    reads=[r_g0buf[1 - p]], writes=[r_g0q[1 - p]])
                b.op("dve", lambda e, p=p: e.tensor_copy(
                    out=g0q[1 - p][:, 1024:2048], in_=g0buf[1 - p][:, 1024:2048]),
                    reads=[r_g0buf[1 - p]], writes=[r_g0q[1 - p]])
                b.op("act", lambda e, p=p: e.copy(
                    out=g1q[1 - p][:, 0:1024], in_=g1buf[1 - p][:, 0:1024]),
                    reads=[r_g1buf[1 - p]], writes=[r_g1q[1 - p]])
                b.op("dve", lambda e, p=p: e.tensor_copy(
                    out=g1q[1 - p][:, 1024:2048], in_=g1buf[1 - p][:, 1024:2048]),
                    reads=[r_g1buf[1 - p]], writes=[r_g1q[1 - p]])
            g0q3 = g0q[p][:].rearrange("p (dc m) -> p dc m", dc=NDC)
            g1q3 = g1q[p][:].rearrange("p (dc m) -> p dc m", dc=NDC)
            for tq in range(4):
                tcol = tq * 512
                ep = tq % 2
                builds = [
                    (lambda e, j=j, tcol=tcol, ep=ep, g0q3=g0q3, q3=q3: e.matmul(
                        out=pfa[ep][:], lhsT=g0q3[:, 2 * j:2 * j + 2, :],
                        rhs=q3[:, 2 * j:2 * j + 2, tcol:tcol + 512],
                        start=(j == 0), stop=(j == NDC // 2 - 1), perf_mode=DR))
                    for j in range(NDC // 2)]
                b.op_group("pe", builds, reads=[r_g0q[p], R["q"]], writes=[r_pfa[ep]])
                builds = [
                    (lambda e, j=j, tcol=tcol, ep=ep, g1q3=g1q3, q3=q3: e.matmul(
                        out=pfb[ep][:], lhsT=g1q3[:, 2 * j:2 * j + 2, :],
                        rhs=q3[:, 2 * j:2 * j + 2, tcol:tcol + 512],
                        start=(j == 0), stop=(j == NDC // 2 - 1), perf_mode=DR))
                    for j in range(NDC // 2)]
                b.op_group("pe", builds, reads=[r_g1q[p], R["q"]], writes=[r_pfb[ep]])
                b.op("dve", lambda e, ep=ep: e.tensor_copy(out=f1b[ep][:], in_=pfb[ep][:]),
                     reads=[r_pfb[ep]], writes=[r_f1b[ep]])
                b.op("act", lambda e, ep=ep: e.activation(
                    out=gateb[ep][:], in_=pfa[ep][:], func=AF.Gelu_apprx_tanh),
                    reads=[r_pfa[ep]], writes=[r_gate[ep]])
                b.op("dve", lambda e, ep=ep: e.tensor_mul(
                    out=prodb[ep][:], in0=gateb[ep][:], in1=f1b[ep][:]),
                    reads=[r_gate[ep], r_f1b[ep]], writes=[r_prod[ep]])
                b.op("act", lambda e, ep=ep, p=p, tcol=tcol: e.copy(
                    out=acts_hc[p][:, tcol:tcol + 512], in_=prodb[ep][:]),
                    reads=[r_prod[ep]], writes=[r_acts_hc[p]])
            for tq in range(4):
                b.dma("gp", acts_d[tq][:, hc * 512:(hc + 1) * 512],
                      acts_hc[p][:, tq * 512:(tq + 1) * 512],
                      reads=[r_acts_hc[p]], writes=[R["acts_dr"]])

        es_q.close()
        acts_sb = [es2.enter_context(nc.sbuf_tensor(f"acts_sb{i}", [128, NHC * 512], E4))
                   for i in range(2)]
        def b_feed(tq, dc):
            pl = dc % 2
            if tq == 0:
                for hq in range(4):
                    p = hq % 2
                    b.dma("sy" if hq < 2 else "gp", lbuf[p][:],
                          wlin_d[dc][:, hq * 2048:(hq + 1) * 2048], writes=[r_lbuf[p]])
                    eng = "act" if hq % 2 == 0 else "dve"
                    fn = ((lambda e, p=p, hq=hq, pl=pl: e.copy(
                        out=lq[pl][:, hq * 2048:(hq + 1) * 2048], in_=lbuf[p][:]))
                        if eng == "act" else
                        (lambda e, p=p, hq=hq, pl=pl: e.tensor_copy(
                            out=lq[pl][:, hq * 2048:(hq + 1) * 2048], in_=lbuf[p][:])))
                    b.op(eng, fn, reads=[r_lbuf[p]], writes=[r_lq[pl]])
                b.dma("gp", wlinq_d[dc], lq[pl][:], reads=[r_lq[pl]], writes=[R["wlinq_dr"]])
            else:
                b.dma("sy", lq[pl][:], wlinq_d[dc], reads=[R["wlinq_dr"]], writes=[r_lq[pl]])

        b.dma("sy", acts_sb[0][:], acts_d[0],
              reads=[R["acts_dr"], R["q"]], writes=[r_acts_sb[0]])
        b_feed(0, 0)
        for tq in range(4):
            tcol = tq * 512
            ab = tq % 2
            if tq + 1 < 4:
                b.dma("sy", acts_sb[1 - ab][:], acts_d[tq + 1],
                      reads=[R["acts_dr"]], writes=[r_acts_sb[1 - ab]])
            acts3 = acts_sb[ab][:].rearrange("p (hc t) -> p hc t", hc=NHC)
            for dc in range(NDC):
                pl = dc % 2
                if dc + 1 < NDC:
                    b_feed(tq, dc + 1)
                elif tq + 1 < 4:
                    b_feed(tq + 1, 0)
                ep = dc % 2
                lq3 = lq[pl][:].rearrange("p (hc m) -> p hc m", hc=NHC)
                builds = [
                    (lambda e, j=j, ep=ep, lq3=lq3, acts3=acts3: e.matmul(
                        out=pfa[ep][:], lhsT=lq3[:, 2 * j:2 * j + 2, :],
                        rhs=acts3[:, 2 * j:2 * j + 2, :],
                        start=(j == 0), stop=(j == NHC // 2 - 1), perf_mode=DR))
                    for j in range(NHC // 2)]
                b.op_group("pe", builds, reads=[r_lq[pl], r_acts_sb[ab]], writes=[r_pfa[ep]])
                b.op("act", lambda e, ep=ep: e.activation(
                    out=gsil[ep][:], in_=pfa[ep][:], func=AF.Silu),
                    reads=[r_pfa[ep]], writes=[r_gsil[ep]])
                b.dma("gp", g_d[dc][:, tcol:tcol + 512], gsil[ep][:],
                      reads=[r_gsil[ep]], writes=[R["g_dr"]])

    es2.close()
    es_q.close()
    b.barrier()

    es3 = ExitStack()
    xnhs = es3.enter_context(nc.sbuf_tensor("xnhs", [128, NDC * 512], BF16))
    xnls = es3.enter_context(nc.sbuf_tensor("xnls", [128, NDC * 512], BF16))
    gs = es3.enter_context(nc.sbuf_tensor("gs", [128, NDC * 512], BF16))
    wbuf = [es3.enter_context(nc.sbuf_tensor(f"wbuf{i}", [128, D], F32)) for i in range(2)]
    wh = [es3.enter_context(nc.sbuf_tensor(f"wh{i}", [128, D], BF16)) for i in range(2)]
    wl = [es3.enter_context(nc.sbuf_tensor(f"wl{i}", [128, D], BF16)) for i in range(2)]
    u_s = es3.enter_context(nc.sbuf_tensor("u_s", [128, NDC * 512], F32))
    ysh = es3.enter_context(nc.sbuf_tensor("ysh", [128, NDC * 512], BF16))
    ysl = es3.enter_context(nc.sbuf_tensor("ysl", [128, NDC * 512], BF16))
    dtv = es3.enter_context(nc.sbuf_tensor("dtv", [32, 512], F32))
    dth = es3.enter_context(nc.sbuf_tensor("dth", [32, 512], BF16))
    dtl = es3.enter_context(nc.sbuf_tensor("dtl", [32, 512], BF16))
    a_t = [es3.enter_context(nc.sbuf_tensor(f"a_t{i}", [128, 512], F32)) for i in range(2)]
    ys_t = [es3.enter_context(nc.sbuf_tensor(f"ys_t{i}", [128, 512], F32)) for i in range(2)]
    z_t = [es3.enter_context(nc.sbuf_tensor(f"z_t{i}", [128, 512], F32)) for i in range(2)]
    xbt = [es3.enter_context(nc.sbuf_tensor(f"xbt{i}", [128, 512], F32)) for i in range(4)]
    ot = [es3.enter_context(nc.sbuf_tensor(f"ot{i}", [128, 512], F32)) for i in range(4)]
    pacc = [es3.enter_context(nc.psum_tensor(f"pacc{i}", [128, 512], F32)) for i in range(2)]
    psb = [es3.enter_context(nc.psum_tensor(f"psb{i}", [128, 512], F32)) for i in range(2)]
    pstb = [es3.enter_context(nc.psum_tensor(f"pstb{i}", [128, 512], F32)) for i in range(4)]

    r_xnhs, r_xnls, r_gs = Res(), Res(), Res()
    r_wbuf = [Res(), Res()]
    r_wh = [Res(), Res()]
    r_wl = [Res(), Res()]
    r_u = Res()
    r_ys = Res()
    r_dt = Res()
    r_a = [Res(), Res()]
    r_yst = [Res(), Res()]
    r_z = [Res(), Res()]
    r_xbt = [Res() for _ in range(4)]
    r_ot = [Res() for _ in range(4)]
    r_pacc = [Res(), Res()]
    r_psb = [Res(), Res()]
    r_pstb = [Res() for _ in range(4)]

    def feed_w(s, ec, which):
        pp = ec % 2
        hd = winh_d[ec] if which == 0 else wouth_d[ec]
        ld = winl_d[ec] if which == 0 else woutl_d[ec]
        rr = R["winhl_dr"] if which == 0 else R["wouthl_dr"]
        if s == 0:
            src = winT_d[ec] if which == 0 else woutT_d[ec]
            b.dma("sy", wbuf[pp][:], src, writes=[r_wbuf[pp]])
            b.op("dve", lambda e, pp=pp: e.tensor_copy(out=wh[pp][:], in_=wbuf[pp][:]),
                 reads=[r_wbuf[pp]], writes=[r_wh[pp]])
            b.op("dve", lambda e, pp=pp: e.tensor_sub(out=wl[pp][:], in0=wbuf[pp][:], in1=wh[pp][:]),
                 reads=[r_wbuf[pp], r_wh[pp]], writes=[r_wl[pp]])
            b.dma("gp", hd, wh[pp][:], reads=[r_wh[pp]], writes=[rr])
            b.dma("gp", ld, wl[pp][:], reads=[r_wl[pp]], writes=[rr])
        else:
            b.dma("sy", wh[pp][:], hd, reads=[rr], writes=[r_wh[pp]])
            b.dma("sy", wl[pp][:], ld, reads=[rr], writes=[r_wl[pp]])

    def mm3(pp, pdst, rdst, rhs_h, rhs_l, r_rh, r_rl):
        builds = []
        nk = 3 * NDC
        k = 0
        for dc in range(NDC):
            for (lhs, rhs) in ((wh[pp], rhs_h), (wh[pp], rhs_l), (wl[pp], rhs_h)):
                builds.append(lambda e, lhs=lhs, rhs=rhs, dc=dc, k=k: e.matmul(
                    out=pdst[:], lhsT=lhs[:, dc * 128:(dc + 1) * 128],
                    rhs=rhs[:, dc * 512:(dc + 1) * 512],
                    start=(k == 0), stop=(k == nk - 1)))
                k += 1
        b.op_group("pe", builds, reads=[r_wh[pp], r_wl[pp], r_rh, r_rl], writes=[rdst])

    def load_xn_slice(s):
        b.dma("sy", xnhs[:], xnh_d[:, :, s * 512:(s + 1) * 512].rearrange("dc p t -> p dc t"),
              reads=[R["xnh_dr"]], writes=[r_xnhs])
        b.dma("sy", xnls[:], xnl_d[:, :, s * 512:(s + 1) * 512].rearrange("dc p t -> p dc t"),
              reads=[R["xnl_dr"]], writes=[r_xnls])

    def load_g_slice(s):
        b.dma("gp", gs[:], g_d[:, :, s * 512:(s + 1) * 512].rearrange("dc p t -> p dc t"),
              reads=[R["g_dr"]], writes=[r_gs])

    if 3 in phases:
        load_xn_slice(0)
        load_g_slice(0)
    for s in range(NS if 3 in phases else 0):
        builds = []
        nk = 3 * NDC
        k = 0
        for dc in range(NDC):
            for (lhs, rhs) in ((wct_h, xnhs), (wct_h, xnls), (wct_l, xnhs)):
                builds.append(lambda e, lhs=lhs, rhs=rhs, dc=dc, k=k: e.matmul(
                    out=psb[0][0:32, :], lhsT=lhs[:, dc * 32:(dc + 1) * 32],
                    rhs=rhs[:, dc * 512:(dc + 1) * 512],
                    start=(k == 0), stop=(k == nk - 1)))
                k += 1
        b.op_group("pe", builds, reads=[R["wct_h"], R["wct_l"], r_xnhs, r_xnls],
                   writes=[r_psb[0]])
        b.op("act", lambda e: e.activation(out=dtv[:], in_=psb[0][0:32, :], func=AF.Exp,
                                           bias=bdt_t[:]),
             reads=[r_psb[0], R["bdt_t"]], writes=[r_dt])
        b.op("act", lambda e: e.activation(out=dtv[:], in_=dtv[:], func=AF.Ln, bias=onet[:]),
             reads=[r_dt, R["onet"]], writes=[r_dt])
        b.op("dve", lambda e: e.tensor_copy(out=dth[:], in_=dtv[:]),
             reads=[r_dt], writes=[r_dt])
        b.op("dve", lambda e: e.tensor_sub(out=dtl[:], in0=dtv[:], in1=dth[:]),
             reads=[r_dt], writes=[r_dt])

        def scan_chain(ct):
            pp = ct % 2
            b.op_group("pe", [
                lambda e, ct=ct, pp=pp: e.matmul(
                    out=psb[pp][:], lhsT=bsel_t[:, ct * 128:(ct + 1) * 128], rhs=dth[:],
                    start=True, stop=False),
                lambda e, ct=ct, pp=pp: e.matmul(
                    out=psb[pp][:], lhsT=bsel_t[:, ct * 128:(ct + 1) * 128], rhs=dtl[:],
                    start=False, stop=True)],
                reads=[R["bsel_t"], r_dt], writes=[r_psb[pp]])
            b.op("act", lambda e, ct=ct, pp=pp: e.activation(
                out=a_t[pp][:], in_=psb[pp][:], func=AF.Exp, scale=negA[:, ct:ct + 1]),
                reads=[r_psb[pp], R["negA"]], writes=[r_a[pp]])
            if s == 0:
                b.op("dve", lambda e, ct=ct, pp=pp: e.tensor_tensor_scan(
                    out=ys_t[pp][:], data0=a_t[pp][:], data1=u_s[:, ct * 512:(ct + 1) * 512],
                    initial=0.0, op0=ALU.mult, op1=ALU.add),
                    reads=[r_a[pp], r_u], writes=[r_yst[pp]])
            else:
                b.op("dve", lambda e, ct=ct, pp=pp: e.tensor_tensor_scan(
                    out=ys_t[pp][:], data0=a_t[pp][:], data1=u_s[:, ct * 512:(ct + 1) * 512],
                    initial=state[:, ct:ct + 1], op0=ALU.mult, op1=ALU.add),
                    reads=[r_a[pp], r_u, R["state"]], writes=[r_yst[pp]])
            b.op("dve", lambda e, ct=ct, pp=pp: e.tensor_copy(
                out=state[:, ct:ct + 1], in_=ys_t[pp][:, 511:512]),
                reads=[r_yst[pp]], writes=[R["state"]])
            b.op("act", lambda e, ct=ct, pp=pp: e.copy(
                out=ysh[:, ct * 512:(ct + 1) * 512], in_=ys_t[pp][:]),
                reads=[r_yst[pp]], writes=[r_ys])
            b.op("dve", lambda e, ct=ct, pp=pp: e.tensor_sub(
                out=ysl[:, ct * 512:(ct + 1) * 512], in0=ys_t[pp][:],
                in1=ysh[:, ct * 512:(ct + 1) * 512]),
                reads=[r_yst[pp], r_ys], writes=[r_ys])

        feed_w(s, 0, 0)
        for ec in range(NDC):
            if ec + 1 < NDC:
                feed_w(s, ec + 1, 0)
            else:
                feed_w(s, 0, 1)
            pp = ec % 2
            mm3(pp, pacc[pp], r_pacc[pp], xnhs, xnls, r_xnhs, r_xnls)
            b.op("dve", lambda e, ec=ec, pp=pp: e.tensor_copy(
                out=u_s[:, ec * 512:(ec + 1) * 512], in_=pacc[pp][:]),
                reads=[r_pacc[pp]], writes=[r_u])
            scan_chain(ec)
        if s + 1 < NS:
            load_xn_slice(s + 1)

        for eg in range(4):
            for tb4 in range(4):
                row = s * 512 + tb4 * 128
                b.dma("gp", xbt[tb4][:], x_d[row:row + 128, eg * 512:(eg + 1) * 512],
                      writes=[r_xbt[tb4]])
            for e4i in range(4):
                ec = eg * 4 + e4i
                if ec + 1 < NDC:
                    feed_w(s, ec + 1, 1)
                pp = ec % 2
                mm3(pp, pacc[pp], r_pacc[pp], ysh, ysl, r_ys, r_ys)
                b.op("dve", lambda e, ec=ec, pp=pp: e.tensor_tensor(
                    out=z_t[pp][:], in0=pacc[pp][:], in1=gs[:, ec * 512:(ec + 1) * 512],
                    op=ALU.mult),
                    reads=[r_pacc[pp], r_gs], writes=[r_z[pp]])
                builds = []
                for tb4 in range(4):
                    builds.append(lambda e, tb4=tb4, e4i=e4i, pp=pp: e.matmul(
                        out=pstb[tb4][:, e4i * 128:(e4i + 1) * 128],
                        lhsT=z_t[pp][:, tb4 * 128:(tb4 + 1) * 128], rhs=identb[:],
                        is_transpose=True, start=(e4i == 0), stop=(e4i == 3)))
                for tb4, bd in enumerate(builds):
                    b.op("pe", bd, reads=[r_z[pp], R["identb"]], writes=[r_pstb[tb4]])
            for tb4 in range(4):
                row = s * 512 + tb4 * 128
                b.op("dve", lambda e, tb4=tb4: e.tensor_add(
                    out=ot[tb4][:], in0=pstb[tb4][:], in1=xbt[tb4][:]),
                    reads=[r_pstb[tb4], r_xbt[tb4]], writes=[r_ot[tb4]])
                b.dma("gp", out_d[row:row + 128, eg * 512:(eg + 1) * 512], ot[tb4][:],
                      reads=[r_ot[tb4]], writes=[R["out_dr"]])
        if s + 1 < NS:
            load_g_slice(s + 1)

    es3.close()

    for _i in range(NQ):
        b._wait("gpsimd", f"gp{_i}", b.cnt[f"gp{_i}"])
        b._wait("sync", f"sy{_i}", b.cnt[f"sy{_i}"])

    with nc.Block() as block:
        @block.tensor
        def _(e):
            for f in b.cmds["tensor"]:
                f(e)

        @block.vector
        def _(e):
            for f in b.cmds["vector"]:
                f(e)

        @block.scalar
        def _(e):
            for f in b.cmds["scalar"]:
                f(e)

        @block.sync
        def _(e):
            for f in b.cmds["sync"]:
                f(e)

        @block.gpsimd
        def _(e):
            for f in b.cmds["gpsimd"]:
                f(e)

    es_g2.close()
    es_glob.close()
    return nc


def prepare_inputs(L0_attn_x, L0_attn_W_in, L0_attn_W_out, L0_attn_log_A, L0_attn_W_dt,
                   L0_attn_b_dt, L0_ffn_w_gating, L0_ffn_w_linear):
    f32 = np.float32
    x = np.ascontiguousarray(L0_attn_x, dtype=f32)

    def blk_cols(w):
        K, M = w.shape
        nm = M // 128
        nk = K // 128
        out = np.empty((nm, 128, K), dtype=f32)
        for mc in range(nm):
            out[mc] = np.ascontiguousarray(
                w[:, mc * 128:(mc + 1) * 128].reshape(nk, 128, 128).transpose(1, 0, 2)
                .reshape(128, K))
        return out

    WinT = np.ascontiguousarray(L0_attn_W_in.T, dtype=f32)
    WoutT = np.ascontiguousarray(L0_attn_W_out.T, dtype=f32)
    winT = blk_cols(WinT)
    woutT = blk_cols(WoutT)
    winraw = np.ascontiguousarray(L0_attn_W_in, dtype=f32)
    wg0 = blk_cols(np.ascontiguousarray(L0_ffn_w_gating[0], dtype=f32))
    wg1 = blk_cols(np.ascontiguousarray(L0_ffn_w_gating[1], dtype=f32))
    wlin = blk_cols(np.ascontiguousarray(L0_ffn_w_linear, dtype=f32))

    WD = np.ascontiguousarray(L0_attn_W_dt.T, dtype=f32)
    wdtT = np.ascontiguousarray(WD.reshape(16, 128, 32).transpose(1, 0, 2).reshape(128, 512))
    bdt = np.ascontiguousarray(L0_attn_b_dt.reshape(32, 1), dtype=f32)
    alog = np.ascontiguousarray(
        L0_attn_log_A.reshape(16, 2, 64).transpose(1, 2, 0).reshape(128, 16), dtype=f32)

    bsel = np.zeros((32, 16 * 128), dtype=ml_dtypes.bfloat16)
    for ct in range(16):
        for p in range(128):
            bsel[2 * ct + p // 64, ct * 128 + p] = 1.0
    ident = np.eye(128, dtype=f32)

    shared = dict(winT=winT, woutT=woutT, winraw=winraw, wdtT=wdtT, bdt=bdt, alog=alog,
                  bsel=bsel, ident=ident, wg0=wg0, wg1=wg1, wlin=wlin)
    return x, shared


_NC_CACHE = []


def kernel(L0_attn_x, L0_attn_W_in, L0_attn_W_out, L0_attn_log_A, L0_attn_W_dt,
           L0_attn_b_dt, L0_ffn_w_gating, L0_ffn_w_linear):
    x, shared = prepare_inputs(L0_attn_x, L0_attn_W_in, L0_attn_W_out, L0_attn_log_A,
                               L0_attn_W_dt, L0_attn_b_dt, L0_ffn_w_gating, L0_ffn_w_linear)
    if not _NC_CACHE:
        _NC_CACHE.append(build_nc())
    nc = _NC_CACHE[0]
    in_maps = [dict(shared, x=np.ascontiguousarray(x[i])) for i in range(8)]
    res = run_bass_kernel_spmd(nc, in_maps, core_ids=list(range(8)))
    return np.stack([np.asarray(r["out"]) for r in res.results], axis=0)


# version 16
# speedup vs baseline: 1.0556x; 1.0302x over previous
import os
import numpy as np
import ml_dtypes
from contextlib import ExitStack

import concourse.bass as bass
import concourse.mybir as mybir
from concourse.bass_utils import run_bass_kernel_spmd

F32 = mybir.dt.float32
BF16 = mybir.dt.bfloat16
E4 = mybir.dt.float8e4
AF = mybir.ActivationFunctionType
ALU = mybir.AluOpType
DR = mybir.MatmulPerfMode.DoubleRow

T = 2048
D = 2048
HID = 8192
NDC = D // 128
NHC = HID // 128
NS = 4
NTH = 2
NTBQ = 4

NQ = 4
ENG_OF = {"pe": "tensor", "dve": "vector", "act": "scalar"}
for _i in range(NQ):
    ENG_OF[f"sy{_i}"] = "sync"
    ENG_OF[f"gp{_i}"] = "gpsimd"
ALL_SEMS = list(ENG_OF.keys())


class Res:
    __slots__ = ("w", "r")

    def __init__(self):
        self.w = {}
        self.r = {}


class Builder:
    def __init__(self):
        self.cmds = {e: [] for e in ("tensor", "vector", "scalar", "sync", "gpsimd")}
        self.cnt = {s: 0 for s in ALL_SEMS + ["bar"]}
        self.sem = {}
        self.waited = {e: {} for e in self.cmds}
        self.nbar = 0
        self.rr = {"sy": 0, "gp": 0}

    def _wait(self, eng, sem, val):
        if val <= 0:
            return
        if self.waited[eng].get(sem, 0) >= val:
            return
        self.waited[eng][sem] = val
        s = self.sem[sem]
        self.cmds[eng].append(lambda e, s=s, v=val: e.wait_ge(s, v))

    def _edges(self, eng, reads, writes):
        for r in reads:
            for sm, v in r.w.items():
                self._wait(eng, sm, v)
        for r in writes:
            for sm, v in r.r.items():
                self._wait(eng, sm, v)
            for sm, v in r.w.items():
                self._wait(eng, sm, v)

    def _record(self, sem, v, reads, writes):
        for r in reads:
            r.r[sem] = v
        for r in writes:
            r.w = {sem: v}
            r.r = {}

    def op(self, sem, build, reads=(), writes=()):
        eng = ENG_OF[sem]
        self._edges(eng, reads, writes)
        self.cnt[sem] += 1
        v = self.cnt[sem]
        s = self.sem[sem]
        self.cmds[eng].append(lambda e, b=build, s=s: b(e).then_inc(s, 1))
        self._record(sem, v, reads, writes)
        return v

    def op_group(self, sem, builds, reads=(), writes=()):
        eng = ENG_OF[sem]
        self._edges(eng, reads, writes)
        self.cnt[sem] += 1
        v = self.cnt[sem]
        s = self.sem[sem]
        n = len(builds)
        for i, bd in enumerate(builds):
            if i == n - 1:
                self.cmds[eng].append(lambda e, b=bd, s=s: b(e).then_inc(s, 1))
            else:
                self.cmds[eng].append(lambda e, b=bd: b(e))
        self._record(sem, v, reads, writes)
        return v

    def dma(self, q, out_ap, in_ap, reads=(), writes=()):
        sem = f"{q}{self.rr[q]}"
        self.rr[q] = (self.rr[q] + 1) % NQ
        eng = ENG_OF[sem]
        self._wait(eng, sem, self.cnt[sem])
        self._edges(eng, reads, writes)
        self.cnt[sem] += 16
        v = self.cnt[sem]
        s = self.sem[sem]
        self.cmds[eng].append(
            lambda e, o=out_ap, i=in_ap, s=s: e.dma_start(out=o, in_=i).then_inc(s, 16))
        self._record(sem, v, reads, writes)
        return v

    def barrier(self):
        self.nbar += 1
        tgt = 5 * self.nbar
        bar = self.sem["bar"]
        for eng in self.cmds:
            for sm in ALL_SEMS:
                if ENG_OF[sm] == eng:
                    self._wait(eng, sm, self.cnt[sm])
            self.cmds[eng].append(lambda e, b=bar: e.sem_inc(b, 1))
            self._wait(eng, "bar", tgt)


def build_nc(phases=(0, 1, 2, 3)):
    nc = bass.Bass("TRN2", target_bir_lowering=False)

    x_d = nc.dram_tensor("x", [T, D], F32, kind="ExternalInput").ap()
    winT_d = nc.dram_tensor("winT", [NDC, 128, D], F32, kind="ExternalInput").ap()
    woutT_d = nc.dram_tensor("woutT", [NDC, 128, D], F32, kind="ExternalInput").ap()
    winraw_d = nc.dram_tensor("winraw", [D, D], F32, kind="ExternalInput").ap()
    wdtT_d = nc.dram_tensor("wdtT", [128, NDC * 32], F32, kind="ExternalInput").ap()
    bdt_d = nc.dram_tensor("bdt", [32, 1], F32, kind="ExternalInput").ap()
    alog_d = nc.dram_tensor("alog", [128, 16], F32, kind="ExternalInput").ap()
    bsel_d = nc.dram_tensor("bsel", [32, 16 * 128], BF16, kind="ExternalInput").ap()
    ident_d = nc.dram_tensor("ident", [128, 128], F32, kind="ExternalInput").ap()
    wg0_d = nc.dram_tensor("wg0", [NHC, 128, D], F32, kind="ExternalInput").ap()
    wg1_d = nc.dram_tensor("wg1", [NHC, 128, D], F32, kind="ExternalInput").ap()
    wlin_d = nc.dram_tensor("wlin", [NDC, 128, HID], F32, kind="ExternalInput").ap()
    out_d = nc.dram_tensor("out", [T, D], F32, kind="ExternalOutput").ap()
    xnh_d = nc.dram_tensor("xnh_s", [NDC, 128, T], BF16, kind="Internal").ap()
    xnl_d = nc.dram_tensor("xnl_s", [NDC, 128, T], BF16, kind="Internal").ap()
    g_d = nc.dram_tensor("g_sc", [NDC, 128, T], BF16, kind="Internal").ap()
    acts_d = nc.dram_tensor("acts_s", [NHC, 128, T], E4, kind="Internal").ap()
    wlinq_d = nc.dram_tensor("wlinq_s", [NDC, 128, HID], E4, kind="Internal").ap()
    winh_d = nc.dram_tensor("winh_s", [NDC, 128, D], BF16, kind="Internal").ap()
    winl_d = nc.dram_tensor("winl_s", [NDC, 128, D], BF16, kind="Internal").ap()
    wouth_d = nc.dram_tensor("wouth_s", [NDC, 128, D], BF16, kind="Internal").ap()
    woutl_d = nc.dram_tensor("woutl_s", [NDC, 128, D], BF16, kind="Internal").ap()

    b = Builder()
    es_glob = ExitStack()
    for sname in ALL_SEMS + ["bar"]:
        b.sem[sname] = es_glob.enter_context(nc.semaphore("S_" + sname))

    es_g2 = ExitStack()
    identb = es_g2.enter_context(nc.sbuf_tensor("identb", [128, 128], F32))
    wdt_h = es_g2.enter_context(nc.sbuf_tensor("wdt_h", [128, NDC * 32], BF16))
    wdt_l = es_g2.enter_context(nc.sbuf_tensor("wdt_l", [128, NDC * 32], BF16))
    wct_h = es_g2.enter_context(nc.sbuf_tensor("wct_h", [128, NDC * 32], BF16))
    wct_l = es_g2.enter_context(nc.sbuf_tensor("wct_l", [128, NDC * 32], BF16))
    bdt_t = es_g2.enter_context(nc.sbuf_tensor("bdt_t", [32, 1], F32))
    negA = es_g2.enter_context(nc.sbuf_tensor("negA", [128, 16], F32))
    bsel_t = es_g2.enter_context(nc.sbuf_tensor("bsel_t", [32, 16 * 128], BF16))
    epst = es_g2.enter_context(nc.sbuf_tensor("epst", [128, 1], F32))
    onet = es_g2.enter_context(nc.sbuf_tensor("onet", [32, 1], F32))
    state = es_g2.enter_context(nc.sbuf_tensor("state", [128, 16], F32))

    R = {}
    for nm in ("identb", "wdt_h", "wdt_l", "wct_h", "wct_l", "bdt_t", "negA",
               "bsel_t", "epst", "onet", "state", "q", "acts", "xnh_dr", "xnl_dr",
               "g_dr", "out_dr", "wgq_dr", "wlinq_dr", "winhl_dr", "wouthl_dr"):
        R[nm] = Res()

    b.dma("sy", identb[:], ident_d, writes=[R["identb"]])
    b.dma("sy", bdt_t[:], bdt_d, writes=[R["bdt_t"]])
    b.dma("sy", negA[:], alog_d, writes=[R["negA"]])
    b.dma("sy", bsel_t[:], bsel_d, writes=[R["bsel_t"]])
    b.op("dve", lambda e: e.memset(epst[:], 1e-5), writes=[R["epst"]])
    b.op("dve", lambda e: e.memset(onet[:], 1.0), writes=[R["onet"]])
    b.op("act", lambda e: e.activation(out=negA[:], in_=negA[:], func=AF.Exp),
         reads=[R["negA"]], writes=[R["negA"]])
    b.op("dve", lambda e: e.tensor_scalar_mul(out=negA[:], in0=negA[:], scalar1=-1.0),
         reads=[R["negA"]], writes=[R["negA"]])

    if 0 in phases:
        es0 = ExitStack()
        wdt_f = es0.enter_context(nc.sbuf_tensor("wdt_f", [128, NDC * 32], F32))
        w0buf = [es0.enter_context(nc.sbuf_tensor(f"w0b{i}", [128, D], F32)) for i in range(2)]
        w0h = [es0.enter_context(nc.sbuf_tensor(f"w0h{i}", [128, D], BF16)) for i in range(2)]
        w0l = [es0.enter_context(nc.sbuf_tensor(f"w0l{i}", [128, D], BF16)) for i in range(2)]
        wc_f = es0.enter_context(nc.sbuf_tensor("wc_f", [32, D], F32))
        wct_f = es0.enter_context(nc.sbuf_tensor("wct_f", [128, NDC * 32], F32))
        pwc = [es0.enter_context(nc.psum_tensor(f"pwc{i}", [32, 512], F32)) for i in range(4)]
        pwt = es0.enter_context(nc.psum_tensor("pwt", [128, 512], F32))
        r_wdtf = Res()
        r_w0buf = [Res(), Res()]
        r_w0h = [Res(), Res()]
        r_w0l = [Res(), Res()]
        r_wcf = Res()
        r_wctf = Res()
        r_pwc = [Res() for _ in range(4)]
        r_pwt = Res()

        b.dma("sy", wdt_f[:], wdtT_d, writes=[r_wdtf])
        b.op("dve", lambda e: e.tensor_copy(out=wdt_h[:], in_=wdt_f[:]),
             reads=[r_wdtf], writes=[R["wdt_h"]])
        b.op("dve", lambda e: e.tensor_sub(out=wdt_l[:], in0=wdt_f[:], in1=wdt_h[:]),
             reads=[r_wdtf, R["wdt_h"]], writes=[R["wdt_l"]])
        for ns in range(4):
            builds = []
            k = 0
            nk = 3 * NDC
            for ec in range(NDC):
                p = ec % 2
                for (lhs, rhs) in ((wdt_h, w0h[p]), (wdt_h, w0l[p]), (wdt_l, w0h[p])):
                    builds.append(lambda e, lhs=lhs, rhs=rhs, ec=ec, ns=ns, k=k: e.matmul(
                        out=pwc[ns][:], lhsT=lhs[:, ec * 32:(ec + 1) * 32],
                        rhs=rhs[:, ns * 512:(ns + 1) * 512],
                        start=(k == 0), stop=(k == nk - 1)))
                    k += 1
            if ns == 0:
                wc_builds = [builds]
            else:
                wc_builds.append(builds)
        for ec in range(NDC):
            p = ec % 2
            b.dma("sy", w0buf[p][:], winraw_d[ec * 128:(ec + 1) * 128, :], writes=[r_w0buf[p]])
            b.op("dve", lambda e, p=p: e.tensor_copy(out=w0h[p][:], in_=w0buf[p][:]),
                 reads=[r_w0buf[p]], writes=[r_w0h[p]])
            b.op("dve", lambda e, p=p: e.tensor_sub(out=w0l[p][:], in0=w0buf[p][:], in1=w0h[p][:]),
                 reads=[r_w0buf[p], r_w0h[p]], writes=[r_w0l[p]])
            for ns in range(4):
                b.op_group("pe", wc_builds[ns][ec * 3:(ec + 1) * 3],
                           reads=[R["wdt_h"], R["wdt_l"], r_w0h[p], r_w0l[p]],
                           writes=[r_pwc[ns]])
        for ns in range(4):
            b.op("dve", lambda e, ns=ns: e.tensor_copy(out=wc_f[:, ns * 512:(ns + 1) * 512],
                                                       in_=pwc[ns][:]),
                 reads=[r_pwc[ns]], writes=[r_wcf])
        builds = []
        for ec in range(NDC):
            builds.append(lambda e, ec=ec: e.matmul(
                out=pwt[:, ec * 32:(ec + 1) * 32], lhsT=wc_f[:, ec * 128:(ec + 1) * 128],
                rhs=identb[0:32, 0:32], is_transpose=True,
                start=(ec == 0), stop=(ec == NDC - 1)))
        b.op_group("pe", builds, reads=[r_wcf, R["identb"]], writes=[r_pwt])
        b.op("dve", lambda e: e.tensor_copy(out=wct_f[:], in_=pwt[:]),
             reads=[r_pwt], writes=[r_wctf])
        b.op("dve", lambda e: e.tensor_copy(out=wct_h[:], in_=wct_f[:]),
             reads=[r_wctf], writes=[R["wct_h"]])
        b.op("dve", lambda e: e.tensor_sub(out=wct_l[:], in0=wct_f[:], in1=wct_h[:]),
             reads=[r_wctf, R["wct_h"]], writes=[R["wct_l"]])
        es0.close()

    b.barrier()

    es_q = ExitStack()
    q_t = es_q.enter_context(nc.sbuf_tensor("q_t", [128, NDC * T], E4, side="right"))

    es1 = ExitStack()
    xt = [es1.enter_context(nc.sbuf_tensor(f"xt{i}", [128, D], F32)) for i in range(4)]
    xn_t = [es1.enter_context(nc.sbuf_tensor(f"xn{i}", [128, D], F32)) for i in range(8)]
    bns = es1.enter_context(nc.sbuf_tensor("bns", [128, 8 * 6 * 4], F32))
    mv = es1.enter_context(nc.sbuf_tensor("mv", [128, 16], F32))
    rstd = es1.enter_context(nc.sbuf_tensor("rstd", [128, 8], F32))
    nmr = es1.enter_context(nc.sbuf_tensor("nmr", [128, 8], F32))
    xnh_st = [es1.enter_context(nc.sbuf_tensor(f"xnh_st{i}", [128, NDC * 512], BF16)) for i in range(2)]
    xnl_st = [es1.enter_context(nc.sbuf_tensor(f"xnl_st{i}", [128, NDC * 512], BF16)) for i in range(2)]
    ps1 = [es1.enter_context(nc.psum_tensor(f"ps1_{i}", [128, 512], F32)) for i in range(4)]

    r_xt = [Res() for _ in range(4)]
    r_xn = [Res() for _ in range(8)]
    r_stats = [Res() for _ in range(8)]
    r_xnst = [Res(), Res()]
    r_ps1 = [Res() for _ in range(4)]

    for tbq in range(NTBQ if 1 in phases else 0):
        tp_ = tbq % 2
        for tb4 in range(4):
            tb = tbq * 4 + tb4
            p = tb % 4
            sl = tb4 + 4 * tp_
            b.dma("sy", xt[p][:], x_d[tb * 128:(tb + 1) * 128, :], writes=[r_xt[p]])
            for g in range(4):
                b.op("dve", lambda e, p=p, sl=sl, g=g: e.bn_stats(
                    out=bns[:, (sl * 4 + g) * 6:(sl * 4 + g + 1) * 6],
                    in_=xt[p][:, g * 512:(g + 1) * 512]),
                    reads=[r_xt[p]], writes=[r_stats[sl]])
            b.op("dve", lambda e, sl=sl: e.bn_aggr(
                out=mv[:, sl * 2:sl * 2 + 2], in_=bns[:, sl * 24:(sl + 1) * 24]),
                reads=[r_stats[sl]], writes=[r_stats[sl]])
            b.op("act", lambda e, sl=sl: e.activation(
                out=rstd[:, sl:sl + 1], in_=mv[:, sl * 2 + 1:sl * 2 + 2],
                func=AF.Sqrt, bias=epst[:]),
                reads=[r_stats[sl], R["epst"]], writes=[r_stats[sl]])
            b.op("dve", lambda e, sl=sl: e.reciprocal(
                out=rstd[:, sl:sl + 1], in_=rstd[:, sl:sl + 1]),
                reads=[r_stats[sl]], writes=[r_stats[sl]])
            b.op("dve", lambda e, sl=sl: e.tensor_mul(
                out=nmr[:, sl:sl + 1], in0=mv[:, sl * 2:sl * 2 + 1],
                in1=rstd[:, sl:sl + 1]),
                reads=[r_stats[sl]], writes=[r_stats[sl]])
            b.op("dve", lambda e, sl=sl: e.tensor_scalar_mul(
                out=nmr[:, sl:sl + 1], in0=nmr[:, sl:sl + 1], scalar1=-1.0),
                reads=[r_stats[sl]], writes=[r_stats[sl]])
            b.op("act", lambda e, p=p, sl=sl: e.activation(
                out=xn_t[sl][:], in_=xt[p][:], func=AF.Identity,
                scale=rstd[:, sl:sl + 1], bias=nmr[:, sl:sl + 1]),
                reads=[r_xt[p], r_stats[sl]], writes=[r_xn[sl]])
        for dc in range(NDC):
            pp = dc % 2 + 2 * tp_
            builds = []
            for tb4 in range(4):
                builds.append(lambda e, dc=dc, tb4=tb4, pp=pp, tp_=tp_: e.matmul(
                    out=ps1[pp][:, tb4 * 128:(tb4 + 1) * 128],
                    lhsT=xn_t[tb4 + 4 * tp_][:, dc * 128:(dc + 1) * 128],
                    rhs=identb[:], is_transpose=True,
                    start=(tb4 == 0), stop=(tb4 == 3)))
            b.op_group("pe", builds,
                       reads=[r_xn[4 * tp_], r_xn[1 + 4 * tp_], r_xn[2 + 4 * tp_],
                              r_xn[3 + 4 * tp_], R["identb"]],
                       writes=[r_ps1[pp]])
            b.op("act", lambda e, dc=dc, pp=pp, tp_=tp_: e.copy(
                out=xnh_st[tp_][:, dc * 512:(dc + 1) * 512], in_=ps1[pp][:]),
                reads=[r_ps1[pp]], writes=[r_xnst[tp_]])
            b.op("dve", lambda e, dc=dc, pp=pp, tp_=tp_: e.tensor_sub(
                out=xnl_st[tp_][:, dc * 512:(dc + 1) * 512], in0=ps1[pp][:],
                in1=xnh_st[tp_][:, dc * 512:(dc + 1) * 512]),
                reads=[r_ps1[pp], r_xnst[tp_]], writes=[r_xnst[tp_]])
            b.op("dve", lambda e, dc=dc, tbq=tbq, pp=pp: e.tensor_copy(
                out=q_t[:, dc * T + tbq * 512: dc * T + (tbq + 1) * 512], in_=ps1[pp][:]),
                reads=[r_ps1[pp]], writes=[R["q"]])
        b.dma("gp", xnh_d[:, :, tbq * 512:(tbq + 1) * 512].rearrange("dc p t -> p dc t"),
              xnh_st[tp_][:], reads=[r_xnst[tp_]], writes=[R["xnh_dr"]])
        b.dma("gp", xnl_d[:, :, tbq * 512:(tbq + 1) * 512].rearrange("dc p t -> p dc t"),
              xnl_st[tp_][:], reads=[r_xnst[tp_]], writes=[R["xnl_dr"]])

    es1.close()
    b.barrier()

    es2b = ExitStack()
    lbuf = [es2b.enter_context(nc.sbuf_tensor(f"lbuf{i}", [128, D], F32)) for i in range(2)]
    lq = [es2b.enter_context(nc.sbuf_tensor(f"lq{i}", [128, HID], E4)) for i in range(2)]
    gsil = [es2b.enter_context(nc.sbuf_tensor(f"gsil{i}", [128, 512], BF16)) for i in range(2)]
    acts_sb = es2b.enter_context(nc.sbuf_tensor("acts_sb", [128, NHC * 512], E4))
    pfa = [es2b.enter_context(nc.psum_tensor(f"pfa{i}", [128, 512], F32)) for i in range(2)]
    pfb = [es2b.enter_context(nc.psum_tensor(f"pfb{i}", [128, 512], F32)) for i in range(2)]
    es2a = ExitStack()
    g0buf = [es2a.enter_context(nc.sbuf_tensor(f"g0b{i}", [128, D], F32)) for i in range(2)]
    g1buf = [es2a.enter_context(nc.sbuf_tensor(f"g1b{i}", [128, D], F32)) for i in range(2)]
    g0q = [es2a.enter_context(nc.sbuf_tensor(f"g0q{i}", [128, D], E4)) for i in range(2)]
    g1q = [es2a.enter_context(nc.sbuf_tensor(f"g1q{i}", [128, D], E4)) for i in range(2)]
    acts_hc = [es2a.enter_context(nc.sbuf_tensor(f"acts{i}", [128, T], E4)) for i in range(2)]
    f1b = [es2a.enter_context(nc.sbuf_tensor(f"f1b{i}", [128, 512], BF16)) for i in range(2)]
    gateb = [es2a.enter_context(nc.sbuf_tensor(f"gate{i}", [128, 512], BF16)) for i in range(2)]
    prodb = [es2a.enter_context(nc.sbuf_tensor(f"prod{i}", [128, 512], BF16)) for i in range(2)]

    r_g0buf = [Res(), Res()]
    r_g1buf = [Res(), Res()]
    r_g0q = [Res(), Res()]
    r_g1q = [Res(), Res()]
    r_acts_hc = [Res(), Res()]
    r_acts_sb = Res()
    r_f1b = [Res(), Res()]
    r_gate = [Res(), Res()]
    r_prod = [Res(), Res()]
    r_lbuf = [Res(), Res()]
    r_lq = [Res(), Res()]
    r_gsil = [Res(), Res()]
    r_pfa = [Res(), Res()]
    r_pfb = [Res(), Res()]
    R["acts_dr"] = Res()

    if 2 in phases:
        q3 = q_t[:].rearrange("p (dc t) -> p dc t", dc=NDC)
        b.dma("sy", g0buf[0][:], wg0_d[0], writes=[r_g0buf[0]])
        b.dma("sy", g1buf[0][:], wg1_d[0], writes=[r_g1buf[0]])
        b.op("act", lambda e: e.copy(out=g0q[0][:, 0:1024], in_=g0buf[0][:, 0:1024]),
             reads=[r_g0buf[0]], writes=[r_g0q[0]])
        b.op("dve", lambda e: e.tensor_copy(out=g0q[0][:, 1024:2048], in_=g0buf[0][:, 1024:2048]),
             reads=[r_g0buf[0]], writes=[r_g0q[0]])
        b.op("act", lambda e: e.copy(out=g1q[0][:, 0:1024], in_=g1buf[0][:, 0:1024]),
             reads=[r_g1buf[0]], writes=[r_g1q[0]])
        b.op("dve", lambda e: e.tensor_copy(out=g1q[0][:, 1024:2048], in_=g1buf[0][:, 1024:2048]),
             reads=[r_g1buf[0]], writes=[r_g1q[0]])
        for hc in range(NHC):
            p = hc % 2
            if hc + 1 < NHC:
                b.dma("sy", g0buf[1 - p][:], wg0_d[hc + 1], writes=[r_g0buf[1 - p]])
                b.dma("sy", g1buf[1 - p][:], wg1_d[hc + 1], writes=[r_g1buf[1 - p]])
                b.op("act", lambda e, p=p: e.copy(
                    out=g0q[1 - p][:, 0:1024], in_=g0buf[1 - p][:, 0:1024]),
                    reads=[r_g0buf[1 - p]], writes=[r_g0q[1 - p]])
                b.op("dve", lambda e, p=p: e.tensor_copy(
                    out=g0q[1 - p][:, 1024:2048], in_=g0buf[1 - p][:, 1024:2048]),
                    reads=[r_g0buf[1 - p]], writes=[r_g0q[1 - p]])
                b.op("act", lambda e, p=p: e.copy(
                    out=g1q[1 - p][:, 0:1024], in_=g1buf[1 - p][:, 0:1024]),
                    reads=[r_g1buf[1 - p]], writes=[r_g1q[1 - p]])
                b.op("dve", lambda e, p=p: e.tensor_copy(
                    out=g1q[1 - p][:, 1024:2048], in_=g1buf[1 - p][:, 1024:2048]),
                    reads=[r_g1buf[1 - p]], writes=[r_g1q[1 - p]])
            g0q3 = g0q[p][:].rearrange("p (dc m) -> p dc m", dc=NDC)
            g1q3 = g1q[p][:].rearrange("p (dc m) -> p dc m", dc=NDC)
            for tq in range(4):
                tcol = tq * 512
                ep = tq % 2
                builds = [
                    (lambda e, j=j, tcol=tcol, ep=ep, g0q3=g0q3, q3=q3: e.matmul(
                        out=pfa[ep][:], lhsT=g0q3[:, 2 * j:2 * j + 2, :],
                        rhs=q3[:, 2 * j:2 * j + 2, tcol:tcol + 512],
                        start=(j == 0), stop=(j == NDC // 2 - 1), perf_mode=DR))
                    for j in range(NDC // 2)]
                b.op_group("pe", builds, reads=[r_g0q[p], R["q"]], writes=[r_pfa[ep]])
                builds = [
                    (lambda e, j=j, tcol=tcol, ep=ep, g1q3=g1q3, q3=q3: e.matmul(
                        out=pfb[ep][:], lhsT=g1q3[:, 2 * j:2 * j + 2, :],
                        rhs=q3[:, 2 * j:2 * j + 2, tcol:tcol + 512],
                        start=(j == 0), stop=(j == NDC // 2 - 1), perf_mode=DR))
                    for j in range(NDC // 2)]
                b.op_group("pe", builds, reads=[r_g1q[p], R["q"]], writes=[r_pfb[ep]])
                b.op("dve", lambda e, ep=ep: e.tensor_copy(out=f1b[ep][:], in_=pfb[ep][:]),
                     reads=[r_pfb[ep]], writes=[r_f1b[ep]])
                b.op("act", lambda e, ep=ep: e.activation(
                    out=gateb[ep][:], in_=pfa[ep][:], func=AF.Gelu_apprx_tanh),
                    reads=[r_pfa[ep]], writes=[r_gate[ep]])
                b.op("dve", lambda e, ep=ep: e.tensor_mul(
                    out=prodb[ep][:], in0=gateb[ep][:], in1=f1b[ep][:]),
                    reads=[r_gate[ep], r_f1b[ep]], writes=[r_prod[ep]])
                b.op("act", lambda e, ep=ep, p=p, tcol=tcol: e.copy(
                    out=acts_hc[p][:, tcol:tcol + 512], in_=prodb[ep][:]),
                    reads=[r_prod[ep]], writes=[r_acts_hc[p]])
            b.dma("gp", acts_d[hc], acts_hc[p][:], reads=[r_acts_hc[p]], writes=[R["acts_dr"]])

        es_q.close()
        es2a.close()

    es3r = ExitStack()
    xnhs = es3r.enter_context(nc.sbuf_tensor("xnhs", [128, NDC * 512], BF16, side="right"))
    xnls = es3r.enter_context(nc.sbuf_tensor("xnls", [128, NDC * 512], BF16, side="right"))
    wbuf = [es3r.enter_context(nc.sbuf_tensor(f"wbuf{i}", [128, D], F32, side="right"))
            for i in range(2)]
    wh = [es3r.enter_context(nc.sbuf_tensor(f"wh{i}", [128, D], BF16, side="right"))
          for i in range(2)]
    wl = [es3r.enter_context(nc.sbuf_tensor(f"wl{i}", [128, D], BF16, side="right"))
          for i in range(2)]
    u_t = [es3r.enter_context(nc.sbuf_tensor(f"u_t{i}", [128, 512], F32, side="right"))
           for i in range(2)]
    ysh = es3r.enter_context(nc.sbuf_tensor("ysh", [128, NDC * 512], BF16, side="right"))
    ysl = es3r.enter_context(nc.sbuf_tensor("ysl", [128, NDC * 512], BF16, side="right"))
    dtv = es3r.enter_context(nc.sbuf_tensor("dtv", [32, 512], F32, side="right"))
    dth = es3r.enter_context(nc.sbuf_tensor("dth", [32, 512], BF16, side="right"))
    dtl = es3r.enter_context(nc.sbuf_tensor("dtl", [32, 512], BF16, side="right"))
    a_t = [es3r.enter_context(nc.sbuf_tensor(f"a_t{i}", [128, 512], F32, side="right"))
           for i in range(2)]
    ys_t = [es3r.enter_context(nc.sbuf_tensor(f"ys_t{i}", [128, 512], F32, side="right"))
            for i in range(2)]
    pacc = [es3r.enter_context(nc.psum_tensor(f"pacc{i}", [128, 512], F32, side="right"))
            for i in range(2)]
    psb = [es3r.enter_context(nc.psum_tensor(f"psb{i}", [128, 512], F32, side="right"))
           for i in range(2)]

    r_xnhs, r_xnls = Res(), Res()
    r_wbuf = [Res(), Res()]
    r_wh = [Res(), Res()]
    r_wl = [Res(), Res()]
    r_u = [Res(), Res()]
    r_ys = Res()
    r_dt = Res()
    r_a = [Res(), Res()]
    r_yst = [Res(), Res()]
    r_pacc = [Res(), Res()]
    r_psb = [Res(), Res()]

    def feed_w(s, ec, which):
        pp = ec % 2
        hd = winh_d[ec] if which == 0 else wouth_d[ec]
        ld = winl_d[ec] if which == 0 else woutl_d[ec]
        rr = R["winhl_dr"] if which == 0 else R["wouthl_dr"]
        if s == 0:
            src = winT_d[ec] if which == 0 else woutT_d[ec]
            b.dma("sy", wbuf[pp][:], src, writes=[r_wbuf[pp], R["q"]])
            b.op("dve", lambda e, pp=pp: e.tensor_copy(out=wh[pp][:], in_=wbuf[pp][:]),
                 reads=[r_wbuf[pp]], writes=[r_wh[pp]])
            b.op("dve", lambda e, pp=pp: e.tensor_sub(out=wl[pp][:], in0=wbuf[pp][:], in1=wh[pp][:]),
                 reads=[r_wbuf[pp], r_wh[pp]], writes=[r_wl[pp]])
            b.dma("gp", hd, wh[pp][:], reads=[r_wh[pp]], writes=[rr])
            b.dma("gp", ld, wl[pp][:], reads=[r_wl[pp]], writes=[rr])
        else:
            b.dma("sy", wh[pp][:], hd, reads=[rr], writes=[r_wh[pp]])
            b.dma("sy", wl[pp][:], ld, reads=[rr], writes=[r_wl[pp]])

    def mm3(pp, pdst, rdst, rhs_h, rhs_l, r_rh, r_rl):
        builds = []
        nk = 3 * NDC
        k = 0
        for dc in range(NDC):
            for (lhs, rhs) in ((wh[pp], rhs_h), (wh[pp], rhs_l), (wl[pp], rhs_h)):
                builds.append(lambda e, lhs=lhs, rhs=rhs, dc=dc, k=k: e.matmul(
                    out=pdst[:], lhsT=lhs[:, dc * 128:(dc + 1) * 128],
                    rhs=rhs[:, dc * 512:(dc + 1) * 512],
                    start=(k == 0), stop=(k == nk - 1)))
                k += 1
        b.op_group("pe", builds, reads=[r_wh[pp], r_wl[pp], r_rh, r_rl], writes=[rdst])

    def load_xn_slice(s):
        b.dma("sy", xnhs[:], xnh_d[:, :, s * 512:(s + 1) * 512].rearrange("dc p t -> p dc t"),
              reads=[R["xnh_dr"]], writes=[r_xnhs, R["q"]])
        b.dma("sy", xnls[:], xnl_d[:, :, s * 512:(s + 1) * 512].rearrange("dc p t -> p dc t"),
              reads=[R["xnl_dr"]], writes=[r_xnls, R["q"]])

    def emit_dt(s):
        builds = []
        nk = 3 * NDC
        k = 0
        for dc in range(NDC):
            for (lhs, rhs) in ((wct_h, xnhs), (wct_h, xnls), (wct_l, xnhs)):
                builds.append(lambda e, lhs=lhs, rhs=rhs, dc=dc, k=k: e.matmul(
                    out=psb[0][0:32, :], lhsT=lhs[:, dc * 32:(dc + 1) * 32],
                    rhs=rhs[:, dc * 512:(dc + 1) * 512],
                    start=(k == 0), stop=(k == nk - 1)))
                k += 1
        b.op_group("pe", builds, reads=[R["wct_h"], R["wct_l"], r_xnhs, r_xnls],
                   writes=[r_psb[0]])
        b.op("act", lambda e: e.activation(out=dtv[:], in_=psb[0][0:32, :], func=AF.Exp,
                                           bias=bdt_t[:]),
             reads=[r_psb[0], R["bdt_t"]], writes=[r_dt])
        b.op("act", lambda e: e.activation(out=dtv[:], in_=dtv[:], func=AF.Ln, bias=onet[:]),
             reads=[r_dt, R["onet"]], writes=[r_dt])
        b.op("dve", lambda e: e.tensor_copy(out=dth[:], in_=dtv[:]),
             reads=[r_dt], writes=[r_dt])
        b.op("dve", lambda e: e.tensor_sub(out=dtl[:], in0=dtv[:], in1=dth[:]),
             reads=[r_dt], writes=[r_dt])

    def scan_chain(s, ct):
        pp = ct % 2
        b.op_group("pe", [
            lambda e, ct=ct, pp=pp: e.matmul(
                out=psb[pp][:], lhsT=bsel_t[:, ct * 128:(ct + 1) * 128], rhs=dth[:],
                start=True, stop=False),
            lambda e, ct=ct, pp=pp: e.matmul(
                out=psb[pp][:], lhsT=bsel_t[:, ct * 128:(ct + 1) * 128], rhs=dtl[:],
                start=False, stop=True)],
            reads=[R["bsel_t"], r_dt], writes=[r_psb[pp]])
        b.op("act", lambda e, ct=ct, pp=pp: e.activation(
            out=a_t[pp][:], in_=psb[pp][:], func=AF.Exp, scale=negA[:, ct:ct + 1]),
            reads=[r_psb[pp], R["negA"]], writes=[r_a[pp]])
        if s == 0:
            b.op("dve", lambda e, ct=ct, pp=pp: e.tensor_tensor_scan(
                out=ys_t[pp][:], data0=a_t[pp][:], data1=u_t[pp][:],
                initial=0.0, op0=ALU.mult, op1=ALU.add),
                reads=[r_a[pp], r_u[pp]], writes=[r_yst[pp]])
        else:
            b.op("dve", lambda e, ct=ct, pp=pp: e.tensor_tensor_scan(
                out=ys_t[pp][:], data0=a_t[pp][:], data1=u_t[pp][:],
                initial=state[:, ct:ct + 1], op0=ALU.mult, op1=ALU.add),
                reads=[r_a[pp], r_u[pp], R["state"]], writes=[r_yst[pp]])
        b.op("dve", lambda e, ct=ct, pp=pp: e.tensor_copy(
            out=state[:, ct:ct + 1], in_=ys_t[pp][:, 511:512]),
            reads=[r_yst[pp]], writes=[R["state"]])
        b.op("act", lambda e, ct=ct, pp=pp: e.copy(
            out=ysh[:, ct * 512:(ct + 1) * 512], in_=ys_t[pp][:]),
            reads=[r_yst[pp]], writes=[r_ys])
        b.op("dve", lambda e, ct=ct, pp=pp: e.tensor_sub(
            out=ysl[:, ct * 512:(ct + 1) * 512], in0=ys_t[pp][:],
            in1=ysh[:, ct * 512:(ct + 1) * 512]),
            reads=[r_yst[pp], r_ys], writes=[r_ys])

    def emit_3a_ec(s, ec):
        if ec + 1 < NDC:
            feed_w(s, ec + 1, 0)
        else:
            feed_w(s, 0, 1)
        pp = ec % 2
        mm3(pp, pacc[pp], r_pacc[pp], xnhs, xnls, r_xnhs, r_xnls)
        b.op("dve", lambda e, ec=ec, pp=pp: e.tensor_copy(
            out=u_t[pp][:], in_=pacc[pp][:]),
            reads=[r_pacc[pp]], writes=[r_u[pp]])
        scan_chain(s, ec)

    if 2 in phases and 3 in phases:
        load_xn_slice(0)
        p3_tasks = [lambda: emit_dt(0), lambda: feed_w(0, 0, 0)]
        p3_tasks += [(lambda ec=ec: emit_3a_ec(0, ec)) for ec in range(NDC)]
        p3_tasks.append(lambda: load_xn_slice(1))
    else:
        p3_tasks = []

    if 2 in phases:
        def b_feed(tq, dc):
            pl = dc % 2
            if tq == 0:
                for hq in range(4):
                    p = hq % 2
                    b.dma("sy", lbuf[p][:], wlin_d[dc][:, hq * 2048:(hq + 1) * 2048],
                          writes=[r_lbuf[p]])
                    eng = "act" if hq % 2 == 0 else "dve"
                    fn = ((lambda e, p=p, hq=hq, pl=pl: e.copy(
                        out=lq[pl][:, hq * 2048:(hq + 1) * 2048], in_=lbuf[p][:]))
                        if eng == "act" else
                        (lambda e, p=p, hq=hq, pl=pl: e.tensor_copy(
                            out=lq[pl][:, hq * 2048:(hq + 1) * 2048], in_=lbuf[p][:])))
                    b.op(eng, fn, reads=[r_lbuf[p]], writes=[r_lq[pl]])
                b.dma("gp", wlinq_d[dc], lq[pl][:], reads=[r_lq[pl]], writes=[R["wlinq_dr"]])
            else:
                b.dma("sy", lq[pl][:], wlinq_d[dc], reads=[R["wlinq_dr"]], writes=[r_lq[pl]])

        b.dma("sy", acts_sb[:],
              acts_d[:, :, 0:512].rearrange("hc p t -> p hc t"),
              reads=[R["acts_dr"], R["q"]], writes=[r_acts_sb])
        b_feed(0, 0)
        it_b = 0
        for tq in range(4):
            tcol = tq * 512
            acts3 = acts_sb[:].rearrange("p (hc t) -> p hc t", hc=NHC)
            for dc in range(NDC):
                pl = dc % 2
                if dc + 1 < NDC:
                    b_feed(tq, dc + 1)
                elif tq + 1 < 4:
                    b_feed(tq + 1, 0)
                ep = dc % 2
                lq3 = lq[pl][:].rearrange("p (hc m) -> p hc m", hc=NHC)
                builds = [
                    (lambda e, j=j, ep=ep, lq3=lq3, acts3=acts3: e.matmul(
                        out=pfa[ep][:], lhsT=lq3[:, 2 * j:2 * j + 2, :],
                        rhs=acts3[:, 2 * j:2 * j + 2, :],
                        start=(j == 0), stop=(j == NHC // 2 - 1), perf_mode=DR))
                    for j in range(NHC // 2)]
                b.op_group("pe", builds, reads=[r_lq[pl], r_acts_sb], writes=[r_pfa[ep]])
                b.op("act", lambda e, ep=ep: e.activation(
                    out=gsil[ep][:], in_=pfa[ep][:], func=AF.Silu),
                    reads=[r_pfa[ep]], writes=[r_gsil[ep]])
                b.dma("gp", g_d[dc][:, tcol:tcol + 512], gsil[ep][:],
                      reads=[r_gsil[ep]], writes=[R["g_dr"]])
                it_b += 1
                if it_b % 3 == 0 and p3_tasks:
                    p3_tasks.pop(0)()
            if tq + 1 < 4:
                b.dma("sy", acts_sb[:], acts_d[:, :, (tq + 1) * 512:(tq + 2) * 512]
                      .rearrange("hc p t -> p hc t"),
                      reads=[R["acts_dr"]], writes=[r_acts_sb])
        while p3_tasks:
            p3_tasks.pop(0)()

    es2b.close()
    b.barrier()

    es3 = ExitStack()
    gs = es3.enter_context(nc.sbuf_tensor("gs", [128, NDC * 512], BF16))
    z_t = [es3.enter_context(nc.sbuf_tensor(f"z_t{i}", [128, 512], F32)) for i in range(2)]
    xbt = [es3.enter_context(nc.sbuf_tensor(f"xbt{i}", [128, 512], F32)) for i in range(4)]
    ot = [es3.enter_context(nc.sbuf_tensor(f"ot{i}", [128, 512], F32)) for i in range(4)]
    pstb = [es3.enter_context(nc.psum_tensor(f"pstb{i}", [128, 512], F32)) for i in range(4)]
    r_gs = Res()
    r_z = [Res(), Res()]
    r_xbt = [Res() for _ in range(4)]
    r_ot = [Res() for _ in range(4)]
    r_pstb = [Res() for _ in range(4)]

    def load_g_slice(s):
        b.dma("gp", gs[:], g_d[:, :, s * 512:(s + 1) * 512].rearrange("dc p t -> p dc t"),
              reads=[R["g_dr"]], writes=[r_gs])

    def emit_3c(s):
        for eg in range(4):
            for tb4 in range(4):
                row = s * 512 + tb4 * 128
                b.dma("gp", xbt[tb4][:], x_d[row:row + 128, eg * 512:(eg + 1) * 512],
                      reads=[R["g_dr"]], writes=[r_xbt[tb4]])
            for e4i in range(4):
                ec = eg * 4 + e4i
                if ec + 1 < NDC:
                    feed_w(s, ec + 1, 1)
                pp = ec % 2
                mm3(pp, pacc[pp], r_pacc[pp], ysh, ysl, r_ys, r_ys)
                b.op("dve", lambda e, ec=ec, pp=pp: e.tensor_tensor(
                    out=z_t[pp][:], in0=pacc[pp][:], in1=gs[:, ec * 512:(ec + 1) * 512],
                    op=ALU.mult),
                    reads=[r_pacc[pp], r_gs], writes=[r_z[pp]])
                builds = []
                for tb4 in range(4):
                    builds.append(lambda e, tb4=tb4, e4i=e4i, pp=pp: e.matmul(
                        out=pstb[tb4][:, e4i * 128:(e4i + 1) * 128],
                        lhsT=z_t[pp][:, tb4 * 128:(tb4 + 1) * 128], rhs=identb[:],
                        is_transpose=True, start=(e4i == 0), stop=(e4i == 3)))
                for tb4, bd in enumerate(builds):
                    b.op("pe", bd, reads=[r_z[pp], R["identb"]], writes=[r_pstb[tb4]])
            for tb4 in range(4):
                row = s * 512 + tb4 * 128
                b.op("dve", lambda e, tb4=tb4: e.tensor_add(
                    out=ot[tb4][:], in0=pstb[tb4][:], in1=xbt[tb4][:]),
                    reads=[r_pstb[tb4], r_xbt[tb4]], writes=[r_ot[tb4]])
                b.dma("gp", out_d[row:row + 128, eg * 512:(eg + 1) * 512], ot[tb4][:],
                      reads=[r_ot[tb4]], writes=[R["out_dr"]])

    if 3 in phases:
        load_g_slice(0)
        emit_3c(0)
        for s in range(1, NS):
            load_g_slice(s)
            emit_dt(s)
            feed_w(s, 0, 0)
            for ec in range(NDC):
                emit_3a_ec(s, ec)
            if s + 1 < NS:
                load_xn_slice(s + 1)
            emit_3c(s)

    es3.close()
    es3r.close()

    for _i in range(NQ):
        b._wait("gpsimd", f"gp{_i}", b.cnt[f"gp{_i}"])
        b._wait("sync", f"sy{_i}", b.cnt[f"sy{_i}"])

    with nc.Block() as block:
        @block.tensor
        def _(e):
            for f in b.cmds["tensor"]:
                f(e)

        @block.vector
        def _(e):
            for f in b.cmds["vector"]:
                f(e)

        @block.scalar
        def _(e):
            for f in b.cmds["scalar"]:
                f(e)

        @block.sync
        def _(e):
            for f in b.cmds["sync"]:
                f(e)

        @block.gpsimd
        def _(e):
            for f in b.cmds["gpsimd"]:
                f(e)

    es_g2.close()
    es_glob.close()
    return nc


def prepare_inputs(L0_attn_x, L0_attn_W_in, L0_attn_W_out, L0_attn_log_A, L0_attn_W_dt,
                   L0_attn_b_dt, L0_ffn_w_gating, L0_ffn_w_linear):
    f32 = np.float32
    x = np.ascontiguousarray(L0_attn_x, dtype=f32)

    def blk_cols(w):
        K, M = w.shape
        nm = M // 128
        nk = K // 128
        out = np.empty((nm, 128, K), dtype=f32)
        for mc in range(nm):
            out[mc] = np.ascontiguousarray(
                w[:, mc * 128:(mc + 1) * 128].reshape(nk, 128, 128).transpose(1, 0, 2)
                .reshape(128, K))
        return out

    WinT = np.ascontiguousarray(L0_attn_W_in.T, dtype=f32)
    WoutT = np.ascontiguousarray(L0_attn_W_out.T, dtype=f32)
    winT = blk_cols(WinT)
    woutT = blk_cols(WoutT)
    winraw = np.ascontiguousarray(L0_attn_W_in, dtype=f32)
    wg0 = blk_cols(np.ascontiguousarray(L0_ffn_w_gating[0], dtype=f32))
    wg1 = blk_cols(np.ascontiguousarray(L0_ffn_w_gating[1], dtype=f32))
    wlin = blk_cols(np.ascontiguousarray(L0_ffn_w_linear, dtype=f32))

    WD = np.ascontiguousarray(L0_attn_W_dt.T, dtype=f32)
    wdtT = np.ascontiguousarray(WD.reshape(16, 128, 32).transpose(1, 0, 2).reshape(128, 512))
    bdt = np.ascontiguousarray(L0_attn_b_dt.reshape(32, 1), dtype=f32)
    alog = np.ascontiguousarray(
        L0_attn_log_A.reshape(16, 2, 64).transpose(1, 2, 0).reshape(128, 16), dtype=f32)

    bsel = np.zeros((32, 16 * 128), dtype=ml_dtypes.bfloat16)
    for ct in range(16):
        for p in range(128):
            bsel[2 * ct + p // 64, ct * 128 + p] = 1.0
    ident = np.eye(128, dtype=f32)

    shared = dict(winT=winT, woutT=woutT, winraw=winraw, wdtT=wdtT, bdt=bdt, alog=alog,
                  bsel=bsel, ident=ident, wg0=wg0, wg1=wg1, wlin=wlin)
    return x, shared


_NC_CACHE = []


def kernel(L0_attn_x, L0_attn_W_in, L0_attn_W_out, L0_attn_log_A, L0_attn_W_dt,
           L0_attn_b_dt, L0_ffn_w_gating, L0_ffn_w_linear):
    x, shared = prepare_inputs(L0_attn_x, L0_attn_W_in, L0_attn_W_out, L0_attn_log_A,
                               L0_attn_W_dt, L0_attn_b_dt, L0_ffn_w_gating, L0_ffn_w_linear)
    if not _NC_CACHE:
        _NC_CACHE.append(build_nc())
    nc = _NC_CACHE[0]
    in_maps = [dict(shared, x=np.ascontiguousarray(x[i])) for i in range(8)]
    res = run_bass_kernel_spmd(nc, in_maps, core_ids=list(range(8)))
    return np.stack([np.asarray(r["out"]) for r in res.results], axis=0)
